# version 21
# baseline (speedup 1.0000x reference)
import math
import contextlib
import numpy as np

import concourse.bass as bass
import concourse.mybir as mybir
from concourse.bass_utils import run_bass_kernel_spmd

F32 = mybir.dt.float32
BF16 = mybir.dt.bfloat16
AF = mybir.ActivationFunctionType
ALU = mybir.AluOpType
AX = mybir.AxisListType

EPS = 1e-5
NCORES = 8


class Cfg:
    def __init__(self, S=2048, D=2048, H=16, NSEQ=2, WH=None):
        self.S, self.D, self.H, self.NSEQ = S, D, H, NSEQ
        self.HD = D // H
        assert self.HD == 128
        self.SCALE = 1.0 / math.sqrt(self.HD)
        self.NCH = D // 128
        self.TG = 512
        assert S % self.TG == 0
        self.NG = S // self.TG
        self.TPG = self.TG // 128
        self.WH = min(4, self.NCH) if WH is None else WH
        self.NKT = S // 128


class Emit:

    def __init__(self, nc, es):
        self.sems = {}
        self.cnt = {}
        for n in ["V", "A", "T", "G"]:
            self.sems[n] = es.enter_context(nc.semaphore("sem_" + n))
            self.cnt[n] = 0

    def inc(self, name, instr, amt=1):
        instr.then_inc(self.sems[name], amt)
        self.cnt[name] += amt
        return instr

    def tok(self, name):
        return (self.sems[name], self.cnt[name])

    def wait(self, eng, name, val=None):
        if val is None:
            val = self.cnt[name]
        if val > 0:
            eng.wait_ge(self.sems[name], val)


def wait_toks(eng, toks):
    best = {}
    for sem, val in toks:
        key = id(sem)
        if key not in best or best[key][1] < val:
            best[key] = (sem, val)
    for sem, val in best.values():
        eng.wait_ge(sem, val)


class Pool:

    def __init__(self, nc, es, prefix, n):
        self.sems = [es.enter_context(nc.semaphore(f"{prefix}{i}")) for i in range(n)]
        self.cnt = [0] * n
        self.i = 0

    def batch(self, eng):
        idx = self.i % len(self.sems)
        self.i += 1
        if self.cnt[idx] > 0:
            eng.wait_ge(self.sems[idx], self.cnt[idx])
        return idx

    def inc(self, idx, instr):
        instr.then_inc(self.sems[idx], 16)
        self.cnt[idx] += 16
        return instr

    def tok(self, idx):
        return (self.sems[idx], self.cnt[idx])

    def wait_all(self, eng):
        for j, s in enumerate(self.sems):
            if self.cnt[j] > 0:
                eng.wait_ge(s, self.cnt[j])


class Buf:

    def __init__(self, nslots):
        self.n = nslots
        self.marks = [[] for _ in range(nslots)]

    def use(self, slot, tok):
        self.marks[slot % self.n].append(tok)

    def wait_reuse(self, eng, slot):
        wait_toks(eng, self.marks[slot % self.n])
        self.marks[slot % self.n] = []


class _RecInstr:
    __slots__ = ("incs",)

    def __init__(self):
        self.incs = []

    def then_inc(self, sem, amt):
        self.incs.append((sem, amt))
        return self


class _Rec:

    def __init__(self):
        self.calls = []

    def __getattr__(self, name):
        def call(*a, **k):
            ri = _RecInstr()
            self.calls.append((name, a, k, ri))
            return ri
        return call

    def replay(self, eng):
        for name, a, k, ri in self.calls:
            instr = getattr(eng, name)(*a, **k)
            for sem, amt in ri.incs:
                instr.then_inc(sem, amt)


class QBlock:

    NAMES = ("sync", "vector", "scalar", "tensor", "gpsimd")

    def __init__(self, block):
        self.block = block
        self.q = {n: [] for n in self.NAMES}

    def _deco(self, n):
        def deco(fn):
            rec = _Rec()
            fn(rec)
            self.q[n].append(rec)
        return deco

    @property
    def sync(self):
        return self._deco("sync")

    @property
    def vector(self):
        return self._deco("vector")

    @property
    def scalar(self):
        return self._deco("scalar")

    @property
    def tensor(self):
        return self._deco("tensor")

    @property
    def gpsimd(self):
        return self._deco("gpsimd")

    def flush(self):
        for n in self.NAMES:
            recs = self.q[n]
            if recs:
                def runner(eng, recs=list(recs)):
                    for rec in recs:
                        rec.replay(eng)
                getattr(self.block, n)(runner)
                self.q[n] = []


def build(cfg: Cfg):
    S, D, H, NSEQ = cfg.S, cfg.D, cfg.H, cfg.NSEQ
    NCH, TG, NG, TPG, WH, NKT = cfg.NCH, cfg.TG, cfg.NG, cfg.TPG, cfg.WH, cfg.NKT
    HD = cfg.HD
    MW = 2 * TG - 128
    NMT = S // TG
    NHALF = NCH // WH
    KG = 2

    nc = bass.Bass("TRN2", target_bir_lowering=False)

    x_h = nc.dram_tensor("x", [NSEQ, S, D], F32, kind="ExternalInput").ap()
    qT_h = nc.dram_tensor("qT", [D, D], F32, kind="ExternalInput").ap()
    kT_h = nc.dram_tensor("kT", [D, D], F32, kind="ExternalInput").ap()
    vT_h = nc.dram_tensor("vT", [D, D], F32, kind="ExternalInput").ap()
    oT_h = nc.dram_tensor("oT", [D, D], F32, kind="ExternalInput").ap()
    W_h = nc.dram_tensor("W", [D, D], F32, kind="ExternalInput").ap()
    qb_h = nc.dram_tensor("qb", [128, NCH], F32, kind="ExternalInput").ap()
    kb_h = nc.dram_tensor("kb", [128, NCH], F32, kind="ExternalInput").ap()
    ob_h = nc.dram_tensor("ob", [128, NCH], F32, kind="ExternalInput").ap()
    vbrow_h = nc.dram_tensor("vbrow", [1, D], F32, kind="ExternalInput").ap()
    lnw_h = nc.dram_tensor("lnw", [128, NCH], F32, kind="ExternalInput").ap()
    lnb_h = nc.dram_tensor("lnb", [128, NCH], F32, kind="ExternalInput").ap()
    id_h = nc.dram_tensor("id128", [128, 128], F32, kind="ExternalInput").ap()
    mask_h = nc.dram_tensor("mask01", [128, MW], F32, kind="ExternalInput").ap()
    y_h = nc.dram_tensor("y", [NSEQ, S, D], F32, kind="ExternalOutput").ap()

    sc_xnT = nc.dram_tensor("sc_xnT", [NSEQ, NG, 128, NCH, 2, TG], BF16).ap()
    sc_uT = nc.dram_tensor("sc_uT", [NSEQ, NG, 128, NCH, 2, TG], BF16).ap()
    sc_QT = nc.dram_tensor("sc_QT", [NSEQ, H, 2, HD, S], BF16).ap()
    sc_KT = nc.dram_tensor("sc_KT", [NSEQ, H, 2, HD, S], BF16).ap()
    sc_V = nc.dram_tensor("sc_V", [NSEQ, H, 128, NKT, 2 * HD], BF16).ap()
    sc_g = nc.dram_tensor("sc_g", [NSEQ, 128, NCH, S], F32).ap()
    sc_Wq = nc.dram_tensor("sc_Wq", [NCH, 128, D], BF16).ap()
    sc_w = {w: nc.dram_tensor(f"sc_w{w}", [NHALF, 128, NCH, 2, WH * 128], BF16).ap()
            for w in ["q", "k", "v"]}
    sc_oT = nc.dram_tensor("sc_oT", [NCH, 128, NCH, 2, 128], BF16).ap()
    sc_negc = nc.dram_tensor("sc_negc", [1, NSEQ * H], F32).ap()
    sc_rc = nc.dram_tensor("sc_rc", [NSEQ, S // 512, H, 512], F32).ap()

    es = contextlib.ExitStack()
    with es:
        em = Emit(nc, es)
        LP = Pool(nc, es, "lp", 20)
        SPa = Pool(nc, es, "spa", 12)
        SPg = Pool(nc, es, "spg", 8)
        OP = Pool(nc, es, "op", 4)
        block = QBlock(es.enter_context(nc.Block()))

        def sb(name, shape, dt=F32, stack=es):
            return stack.enter_context(nc.sbuf_tensor(name, shape, dt))

        ident = sb("ident_s", [128, 128])
        qb = sb("qb_s", [128, NCH])
        kb = sb("kb_s", [128, NCH])
        ob = sb("ob_s", [128, NCH])
        lnw = sb("lnw_s", [128, NCH])
        lnb = sb("lnb_s", [128, NCH])
        lnb_p = sb("lnbp_s", [128, 2, NCH], BF16)
        biasf = sb("biasf_s", [128, NCH])
        vbb = sb("vbb_s", [128, D])
        vbrow = sb("vbrow_s", [1, D])
        onesf = sb("onesf_s", [128, 1])
        onesb = sb("onesb_s", [128, 1], BF16)
        ones_row = sb("onesrow_s", [1, 128])
        normq = sb("normq_s", [1, NSEQ * H * NG])
        normk = sb("normk_s", [1, NSEQ * H * NG])
        vpool_ = sb("vpool_s", [128, 96])
        spool = sb("spool_s", [1, 8])
        negC = sb("negC_s", [1, NSEQ * H])

        cell = {}

        @block.sync
        def _(sync):
            bi = LP.batch(sync)
            for dst, srch in [(ident, id_h), (qb, qb_h),
                              (kb, kb_h), (ob, ob_h), (lnw, lnw_h), (lnb, lnb_h),
                              (vbrow, vbrow_h)]:
                LP.inc(bi, sync.dma_start(out=dst[:], in_=srch[:]))
            cell["t"] = LP.tok(bi)
        tok_misc = cell["t"]

        @block.vector
        def _(v):
            v.memset(onesf[:], 1.0)
            v.memset(ones_row[:], 1.0)
            v.memset(onesb[:], 1.0)
            wait_toks(v, [tok_misc])
            v.drain()
            v.tensor_copy(lnb_p[:, 0, :], lnb[:])
            v.drain()
            em.inc("V", v.tensor_tensor(lnb_p[:, 1, :], lnb[:], lnb_p[:, 0, :], ALU.subtract))
        V_init = em.cnt["V"]

        def phase_barrier():
            block.flush()

            @block.vector
            def _(v):
                em.inc("V", v.drain())

            @block.scalar
            def _(a):
                em.inc("A", a.drain())

            @block.tensor
            def _(t):
                em.inc("T", t.drain())

            @block.gpsimd
            def _(gp):
                em.inc("G", gp.drain())
            marks = {n: em.cnt[n] for n in ["V", "A", "T", "G"]}

            def waits(e, skip=None):
                for n, vv in marks.items():
                    if n != skip:
                        em.wait(e, n, vv)
                LP.wait_all(e)
                SPa.wait_all(e)
                SPg.wait_all(e)
                OP.wait_all(e)

            @block.vector
            def _(v):
                waits(v, "V")

            @block.scalar
            def _(a):
                waits(a, "A")

            @block.tensor
            def _(t):
                waits(t, "T")

            @block.gpsimd
            def _(gp):
                waits(gp, "G")

            @block.sync
            def _(sync):
                waits(sync)
            block.flush()

        wq_store_toks = []
        absacc = sb("w0_acc", [128, 2])
        with contextlib.ExitStack() as pp:
            x_g = [sb(f"p0_x{i}", [128, TPG // 2, D], stack=pp) for i in range(2)]
            xn_g = [sb(f"p0_xn{i}", [128, TPG // 2, D], stack=pp) for i in range(2)]
            v_g = [sb(f"p0_v{i}", [128, TPG // 2, D], stack=pp) for i in range(2)]
            ev = [sb(f"p0_ev{i}", [128, 2, TG // 2], BF16, stack=pp) for i in range(4)]
            ps_t = [pp.enter_context(nc.psum_tensor(f"p0_pt{i}", [128, TG], F32)) for i in range(2)]
            xbuf = Buf(2)
            evbuf = Buf(4)
            bankbuf = Buf(2)
            w_tile = [sb(f"w0_t{i}", [128, D], stack=pp) for i in range(2)]
            w_hi = [sb(f"w0_hi{i}", [128, D], BF16, stack=pp) for i in range(2)]
            w_lo = [sb(f"w0_lo{i}", [128, D], BF16, stack=pp) for i in range(2)]
            ps_b512 = pp.enter_context(nc.psum_tensor("ps_b512", [128, 512], F32))
            wbuf = Buf(2)
            hlbuf = Buf(2)

            for i in range((D + 511) // 512):
                n = min(512, D - i * 512)

                @block.tensor
                def _(t):
                    wait_toks(t, [tok_misc])
                    em.wait(t, "V")
                    em.inc("T", t.matmul(ps_b512[:, 0:n], ones_row[:],
                                         vbrow[:, i * 512:i * 512 + n],
                                         start=True, stop=True))
                t_bc = em.cnt["T"]

                @block.vector
                def _(v):
                    em.wait(v, "T", t_bc)
                    v.drain()
                    em.inc("V", v.tensor_copy(vbb[:, i * 512:i * 512 + n], ps_b512[:, 0:n]))

            _wjob_i = [0]

            def emit_wjob(jid):
                if jid >= 5 * NCH:
                    return
                ci = _wjob_i[0]
                _wjob_i[0] += 1
                if jid < 4 * NCH:
                    wname, wsrc = [("q", qT_h), ("k", kT_h), ("v", vT_h),
                                   ("o", oT_h)][jid // NCH]
                    c = jid % NCH
                else:
                    wname, wsrc = "abs", W_h
                    c = jid - 4 * NCH

                @block.sync
                def _(sync):
                    wbuf.wait_reuse(sync, ci)
                    bi = LP.batch(sync)
                    LP.inc(bi, sync.dma_start(out=w_tile[ci % 2][:],
                                              in_=wsrc[c * 128:(c + 1) * 128, :]))
                    cell["t"] = LP.tok(bi)
                tok_w = cell["t"]

                if wname == "abs":
                    @block.vector
                    def _(v):
                        wait_toks(v, [tok_w])
                        v.drain()
                        v.tensor_reduce(absacc[:, 1:2], w_tile[ci % 2][:], axis=AX.X,
                                        op=ALU.add, apply_absolute_value=True)
                        v.drain()
                        if c == 0:
                            em.inc("V", v.tensor_copy(absacc[:, 0:1], absacc[:, 1:2]))
                        else:
                            em.inc("V", v.tensor_tensor(absacc[:, 0:1], absacc[:, 0:1],
                                                        absacc[:, 1:2], ALU.add))
                        wbuf.use(ci, em.tok("V"))
                    return

                @block.vector
                def _(v):
                    wait_toks(v, [tok_w])
                    hlbuf.wait_reuse(v, ci)
                    v.drain()
                    v.tensor_copy(w_hi[ci % 2][:], w_tile[ci % 2][:])
                    v.drain()
                    em.inc("V", v.tensor_tensor(w_lo[ci % 2][:], w_tile[ci % 2][:],
                                                w_hi[ci % 2][:], ALU.subtract))
                    wbuf.use(ci, em.tok("V"))
                vc = em.cnt["V"]

                @block.gpsimd
                def _(gp):
                    em.wait(gp, "V", vc)
                    bi = SPg.batch(gp)
                    if wname != "o":
                        for half in range(NHALF):
                            lo_ = half * WH * 128
                            hi_ = (half + 1) * WH * 128
                            SPg.inc(bi, gp.dma_start(
                                out=sc_w[wname][half, :, c, 0, :], in_=w_hi[ci % 2][:, lo_:hi_]))
                            SPg.inc(bi, gp.dma_start(
                                out=sc_w[wname][half, :, c, 1, :], in_=w_lo[ci % 2][:, lo_:hi_]))
                    else:
                        for j in range(NCH):
                            SPg.inc(bi, gp.dma_start(
                                out=sc_oT[j, :, c, 0, :], in_=w_hi[ci % 2][:, j * 128:(j + 1) * 128]))
                            SPg.inc(bi, gp.dma_start(
                                out=sc_oT[j, :, c, 1, :], in_=w_lo[ci % 2][:, j * 128:(j + 1) * 128]))
                    hlbuf.use(ci, SPg.tok(bi))
                    wq_store_toks.append(SPg.tok(bi))
            TPS = TPG // 2
            xnvbuf = Buf(2)
            prev_V2 = [0, 0]
            NIT = NSEQ * NG * 2
            jpg = -(-5 * NCH // NIT)
            sg_list = [(s, g, half) for s in range(NSEQ) for g in range(NG) for half in range(2)]

            def stageA(it):
                s, g, half = sg_list[it]
                slot = it % 2
                vp = vpool_[:, (it % 2) * 48:(it % 2) * 48 + 48]
                toff = half * TPS * 128

                @block.sync
                def _(sync):
                    xbuf.wait_reuse(sync, slot)
                    bi = LP.batch(sync)
                    LP.inc(bi, sync.dma_start(
                        out=x_g[slot][:],
                        in_=x_h[s, g * TG + toff:g * TG + toff + TPS * 128, :]
                        .rearrange("(a p) d -> p a d", p=128)))
                    cell["t"] = LP.tok(bi)
                tok_x = cell["t"]

                @block.scalar
                def _(a):
                    wait_toks(a, [tok_x])
                    em.wait(a, "V", prev_V2[slot])
                    xnvbuf.wait_reuse(a, slot)
                    a.drain()
                    for tt in range(TPS):
                        a.activation(xn_g[slot][:, tt, :], x_g[slot][:, tt, :], AF.Square,
                                     accum_out=vp[:, tt:tt + 1])
                    em.inc("A", a.drain())
                a_sq1 = em.cnt["A"]

                @block.vector
                def _(v):
                    wait_toks(v, [tok_x])
                    v.drain()
                    for tt in range(TPS):
                        v.tensor_reduce(vp[:, 4 + tt:5 + tt], x_g[slot][:, tt, :],
                                        axis=AX.X, op=ALU.add)
                    v.drain()
                    v.tensor_scalar(vp[:, 8:8 + TPS], vp[:, 4:4 + TPS], 1.0 / D, None, ALU.mult)
                    em.wait(v, "A", a_sq1)
                    v.tensor_scalar(vp[:, 12:12 + TPS], vp[:, 0:TPS], 1.0 / D, None, ALU.mult)
                    v.drain()
                    v.tensor_tensor(vp[:, 16:16 + TPS], vp[:, 8:8 + TPS],
                                    vp[:, 8:8 + TPS], ALU.mult)
                    v.drain()
                    v.tensor_tensor(vp[:, 12:12 + TPS], vp[:, 12:12 + TPS],
                                    vp[:, 16:16 + TPS], ALU.subtract)
                    v.drain()
                    v.tensor_scalar(vp[:, 12:12 + TPS], vp[:, 12:12 + TPS], EPS, None, ALU.add)
                    v.drain()
                    em.inc("V", v.reciprocal(vp[:, 16:16 + TPS], vp[:, 12:12 + TPS]))
                v_rc1 = em.cnt["V"]

                @block.scalar
                def _(a):
                    em.wait(a, "V", v_rc1)
                    a.drain()
                    em.inc("A", a.activation(vp[:, 16:16 + TPS], vp[:, 16:16 + TPS], AF.Sqrt))
                a_r1 = em.cnt["A"]

                @block.vector
                def _(v):
                    em.wait(v, "A", a_r1)
                    v.drain()
                    em.inc("V", v.scalar_tensor_tensor(vp[:, 20:20 + TPS], vp[:, 8:8 + TPS],
                                                       -1.0, vp[:, 16:16 + TPS],
                                                       ALU.mult, ALU.mult))
                return em.cnt["V"]

            def stageB(it, v_b1):
                s, g, half = sg_list[it]
                slot = it % 2
                vp = vpool_[:, (it % 2) * 48:(it % 2) * 48 + 48]
                toff = half * TPS * 128

                @block.scalar
                def _(a):
                    em.wait(a, "V", v_b1)
                    a.drain()
                    for tt in range(TPS):
                        a.activation(xn_g[slot][:, tt, :], x_g[slot][:, tt, :], AF.Identity,
                                     scale=vp[:, 16 + tt:17 + tt], bias=vp[:, 20 + tt:21 + tt])
                    a.drain()
                    for tt in range(TPS):
                        a.activation(v_g[slot][:, tt, :], xn_g[slot][:, tt, :], AF.Square,
                                     accum_out=vp[:, 24 + tt:25 + tt])
                    em.inc("A", a.drain())
                    xbuf.use(slot, em.tok("A"))
                a_sq2 = em.cnt["A"]

                @block.vector
                def _(v):
                    em.wait(v, "A", a_sq2)
                    v.drain()
                    for tt in range(TPS):
                        v.tensor_reduce(vp[:, 28 + tt:29 + tt], xn_g[slot][:, tt, :],
                                        axis=AX.X, op=ALU.add)
                    v.drain()
                    v.tensor_scalar(vp[:, 32:32 + TPS], vp[:, 28:28 + TPS], 1.0 / D, None, ALU.mult)
                    v.tensor_scalar(vp[:, 36:36 + TPS], vp[:, 24:24 + TPS], 1.0 / D, None, ALU.mult)
                    v.drain()
                    v.tensor_tensor(vp[:, 40:40 + TPS], vp[:, 32:32 + TPS],
                                    vp[:, 32:32 + TPS], ALU.mult)
                    v.drain()
                    v.tensor_tensor(vp[:, 36:36 + TPS], vp[:, 36:36 + TPS],
                                    vp[:, 40:40 + TPS], ALU.subtract)
                    v.drain()
                    v.tensor_scalar(vp[:, 36:36 + TPS], vp[:, 36:36 + TPS], EPS, None, ALU.add)
                    v.drain()
                    em.inc("V", v.reciprocal(vp[:, 40:40 + TPS], vp[:, 36:36 + TPS]))
                v_rc2 = em.cnt["V"]

                @block.scalar
                def _(a):
                    em.wait(a, "V", v_rc2)
                    a.drain()
                    em.inc("A", a.activation(vp[:, 40:40 + TPS], vp[:, 40:40 + TPS], AF.Sqrt))
                a_r2 = em.cnt["A"]

                @block.vector
                def _(v):
                    em.wait(v, "A", a_r2)
                    v.drain()
                    em.inc("V", v.scalar_tensor_tensor(vp[:, 44:44 + TPS], vp[:, 32:32 + TPS],
                                                       -1.0, vp[:, 40:40 + TPS],
                                                       ALU.mult, ALU.mult))
                v_b2 = em.cnt["V"]
                prev_V2[slot] = v_b2

                @block.scalar
                def _(a):
                    em.wait(a, "V", v_b2)
                    a.drain()
                    for tt in range(TPS):
                        a.activation(v_g[slot][:, tt, :], xn_g[slot][:, tt, :], AF.Identity,
                                     scale=vp[:, 40 + tt:41 + tt], bias=vp[:, 44 + tt:45 + tt])
                    em.inc("A", a.drain())
                a_tiles = em.cnt["A"]

                for which, (src_g, dst_sc) in enumerate(
                        [(xn_g, sc_xnT), (v_g, sc_uT)]):
                    for c in range(NCH):
                        bidx = which * NCH + c
                        bank = ps_t[bidx % 2]

                        @block.tensor
                        def _(t):
                            em.wait(t, "A", a_tiles)
                            bankbuf.wait_reuse(t, bidx)
                            for tt in range(TPS):
                                t.transpose(bank[:, tt * 128:(tt + 1) * 128],
                                            src_g[slot][:, tt, c * 128:(c + 1) * 128],
                                            ident[:])
                            em.inc("T", t.drain())
                            xnvbuf.use(slot, em.tok("T"))
                        t_tr = em.cnt["T"]
                        eslot = bidx % 4

                        @block.scalar
                        def _(a):
                            em.wait(a, "T", t_tr)
                            evbuf.wait_reuse(a, eslot)
                            a.drain()
                            if which == 0:
                                a.activation(ev[eslot][:, 0, :], bank[:, 0:TPS * 128],
                                             AF.Identity)
                            else:
                                a.activation(ev[eslot][:, 0, :], bank[:, 0:TPS * 128],
                                             AF.Identity, scale=lnw[:, c:c + 1])
                            em.inc("A", a.drain())
                        a_hi = em.cnt["A"]

                        @block.vector
                        def _(v):
                            em.wait(v, "A", a_hi)
                            v.drain()
                            if which == 0:
                                em.inc("V", v.tensor_tensor(ev[eslot][:, 1, :],
                                                            bank[:, 0:TPS * 128],
                                                            ev[eslot][:, 0, :], ALU.subtract))
                            else:
                                em.inc("V", v.scalar_tensor_tensor(
                                    ev[eslot][:, 1, :], bank[:, 0:TPS * 128],
                                    lnw[:, c:c + 1],
                                    ev[eslot][:, 0, :], ALU.mult, ALU.subtract))
                            bankbuf.use(bidx, em.tok("V"))
                        v_lo = em.cnt["V"]

                        @block.scalar
                        def _(a):
                            em.wait(a, "V", v_lo)
                            bi = SPa.batch(a)
                            SPa.inc(bi, a.dma_start(
                                out=dst_sc[s, g, :, c, :, toff:toff + TPS * 128],
                                in_=ev[eslot][:]))
                            evbuf.use(eslot, SPa.tok(bi))
                for jid in range(it * jpg, (it + 1) * jpg):
                    emit_wjob(jid)
                block.flush()

            vb1 = stageA(0)
            for it in range(NIT):
                nxt = stageA(it + 1) if it + 1 < NIT else None
                stageB(it, vb1)
                vb1 = nxt
        phase_barrier()

        with contextlib.ExitStack() as pp:
            w_tile = [sb(f"wb_t{i}", [128, D], stack=pp) for i in range(2)]
            wq_sb = sb("wb_wq", [128, NCH, D], BF16, stack=pp)
            thr_b = sb("wb_thr", [128, 1], stack=pp)
            mneg = sb("wb_mneg", [128, D], stack=pp)
            ps_w0 = pp.enter_context(nc.psum_tensor("ps_w0", [1, 1], F32))
            ps_thr = pp.enter_context(nc.psum_tensor("ps_thr", [128, 1], F32))
            ps_bias = pp.enter_context(nc.psum_tensor("ps_bias", [128, NCH], F32))
            wbuf = Buf(2)

            @block.tensor
            def _(t):
                em.wait(t, "V")
                em.inc("T", t.matmul(ps_w0[:], absacc[:, 0:1], onesf[:], start=True, stop=True))
            T_sum = em.cnt["T"]

            @block.vector
            def _(v):
                em.wait(v, "T", T_sum)
                v.drain()
                em.inc("V", v.tensor_scalar(spool[:, 0:1], ps_w0[:], 0.5 / (D * D), None, ALU.mult))
            V_thr = em.cnt["V"]

            @block.tensor
            def _(t):
                em.wait(t, "V", V_thr)
                em.inc("T", t.matmul(ps_thr[:], ones_row[:], spool[:, 0:1],
                                     start=True, stop=True))
            T_thr = em.cnt["T"]

            @block.vector
            def _(v):
                em.wait(v, "T", T_thr)
                v.drain()
                em.inc("V", v.tensor_copy(thr_b[:], ps_thr[:]))
            V_thrb = em.cnt["V"]

            for c in range(NCH):
                @block.sync
                def _(sync):
                    wbuf.wait_reuse(sync, c)
                    bi = LP.batch(sync)
                    LP.inc(bi, sync.dma_start(out=w_tile[c % 2][:],
                                              in_=W_h[c * 128:(c + 1) * 128, :]))
                    cell["t"] = LP.tok(bi)
                tok_w = cell["t"]

                @block.vector
                def _(v):
                    wait_toks(v, [tok_w])
                    em.wait(v, "V", V_thrb)
                    v.drain()
                    v.tensor_scalar(mneg[:], w_tile[c % 2][:], thr_b[:], 0.0, ALU.add, ALU.is_lt)
                    v.drain()
                    em.inc("V", v.scalar_tensor_tensor(wq_sb[:, c, :], w_tile[c % 2][:], thr_b[:],
                                                       mneg[:], ALU.is_gt, ALU.subtract))
                    wbuf.use(c, em.tok("V"))
                vc = em.cnt["V"]

                @block.gpsimd
                def _(gp):
                    em.wait(gp, "V", vc)
                    bi = SPg.batch(gp)
                    SPg.inc(bi, gp.dma_start(out=sc_Wq[c], in_=wq_sb[:, c, :]))
                    wq_store_toks.append(SPg.tok(bi))
            V_wq = em.cnt["V"]

            @block.tensor
            def _(t):
                em.wait(t, "V", max(V_wq, V_init))
                for j in range(NCH):
                    for c in range(NCH):
                        for hl in range(2):
                            t.matmul(ps_bias[:, j:j + 1], wq_sb[:, c, j * 128:(j + 1) * 128],
                                     lnb_p[:, hl, c:c + 1],
                                     start=(c == 0 and hl == 0),
                                     stop=(c == NCH - 1 and hl == 1),
                                     skip_group_check=True)
                em.inc("T", t.drain())
            T_bias = em.cnt["T"]

            @block.vector
            def _(v):
                em.wait(v, "T", T_bias)
                v.drain()
                em.inc("V", v.tensor_copy(biasf[:], ps_bias[:]))

            @block.gpsimd
            def _(gp):
                wait_toks(gp, wq_store_toks)
                em.wait(gp, "T", T_bias)
                em.inc("G", gp.drain())

        phase_barrier()

        with contextlib.ExitStack() as pp:
            wt_sb_ = [sb(f"p1_w{i}", [128, NCH, 2, WH * 128], BF16, stack=pp) for i in range(2)]
            act_g = [sb(f"p1_a{i}", [128, NCH, 2, TG], BF16, stack=pp) for i in range(2)]
            st_g = [sb(f"p1_s{i}", [128, 2, TG], BF16, stack=pp) for i in range(4)]
            stv_g = [sb("p1_sv0", [128, TG], F32, stack=pp)]
            stf_g = [sb("p1_sf0", [128, TG], F32, stack=pp)]
            sq_g = [sb(f"p1_sq{i}", [128, TG], BF16, stack=pp) for i in range(4)]
            ps_p = [pp.enter_context(nc.psum_tensor(f"p1_p{i}", [128, TG], F32)) for i in range(2)]
            ps_n = [pp.enter_context(nc.psum_tensor(f"p1_n{i}", [1, TG], F32)) for i in range(2)]
            ps_n_buf = Buf(2)
            abuf = Buf(2)
            stbuf = Buf(4)
            stvbuf = Buf(1)
            stfbuf = Buf(1)
            wbuf = Buf(2)
            pbuf = Buf(2)

            OT = min(512, WH * 128)
            sqbuf = Buf(4)
            pending_norm = []
            nrm_i = [0]

            def emit_pending_norms(drainall=False):
                while (len(pending_norm) > 2) or (drainall and pending_norm):
                    a_ev, sslot, pos, normbuf = pending_norm.pop(0)
                    ni = nrm_i[0]
                    nrm_i[0] += 1

                    @block.tensor
                    def _(t):
                        em.wait(t, "A", a_ev)
                        ps_n_buf.wait_reuse(t, ni)
                        em.inc("T", t.matmul(ps_n[ni % 2][:], onesb[:], sq_g[sslot][:],
                                             start=True, stop=True))
                        sqbuf.use(sslot, em.tok("T"))
                    t_n = em.cnt["T"]

                    @block.vector
                    def _(v):
                        em.wait(v, "T", t_n)
                        v.drain()
                        em.inc("V", v.tensor_reduce(normbuf[:, pos:pos + 1], ps_n[ni % 2][:],
                                                    axis=AX.X, op=ALU.max))
                        ps_n_buf.use(ni, em.tok("V"))

            jobs = [("q", "q", qb, sc_QT, normq, sc_xnT),
                    ("k", "k", kb, sc_KT, normk, sc_xnT),
                    ("v", "v", None, sc_V, None, sc_xnT),
                    ("f", None, None, sc_g, None, sc_uT)]
            st_i = 0
            w_i = 0
            for wname, wkey, bias_cols, dst, normbuf, actsrc in jobs:
                for half in range(NHALF):
                    wslot = w_i % 2
                    wt_sb = wt_sb_[wslot]
                    w_i += 1

                    @block.sync
                    def _(sync):
                        wbuf.wait_reuse(sync, wslot)
                        bi = LP.batch(sync)
                        if wkey is not None:
                            LP.inc(bi, sync.dma_start(out=wt_sb[:], in_=sc_w[wkey][half]))
                        else:
                            LP.inc(bi, sync.dma_start(
                                out=wt_sb[:, :, 0, :],
                                in_=sc_Wq[:, :, half * WH * 128:(half + 1) * WH * 128]
                                .rearrange("c p o -> p c o")))
                        cell["t"] = LP.tok(bi)
                    tok_w = cell["t"]

                    for s in range(NSEQ):
                        for g in range(NG):
                            if (s * NG + g) % 2 == 0:
                                block.flush()
                            aslot = st_i % 2

                            @block.sync
                            def _(sync):
                                abuf.wait_reuse(sync, aslot)
                                bi = LP.batch(sync)
                                LP.inc(bi, sync.dma_start(
                                    out=act_g[aslot][:], in_=actsrc[s, g]))
                                cell["t"] = LP.tok(bi)
                            tok_a = cell["t"]

                            if wname != "v":
                                for j in range(WH):
                                    jj = half * WH + j
                                    pslot = st_i % 2
                                    pb = ps_p[pslot]

                                    emit_pending_norms()

                                    @block.tensor
                                    def _(t):
                                        wait_toks(t, [tok_w, tok_a])
                                        pbuf.wait_reuse(t, pslot)
                                        first = True
                                        for c in range(NCH):
                                            wh = wt_sb[:, c, 0, j * 128:(j + 1) * 128]
                                            wl = wt_sb[:, c, 1, j * 128:(j + 1) * 128]
                                            ah = act_g[aslot][:, c, 0, :]
                                            al = act_g[aslot][:, c, 1, :]
                                            if wname == "f":
                                                t.matmul(pb[:], wh, ah, start=first, stop=False)
                                                mi = t.matmul(pb[:], wh, al, start=False,
                                                              stop=(c == NCH - 1))
                                            else:
                                                t.matmul(pb[:], wh, ah, start=first, stop=False)
                                                t.matmul(pb[:], wh, al, start=False, stop=False)
                                                mi = t.matmul(pb[:], wl, ah, start=False,
                                                              stop=(c == NCH - 1))
                                            first = False
                                        em.inc("T", mi)
                                        abuf.use(aslot, em.tok("T"))
                                        wbuf.use(wslot, em.tok("T"))
                                    t_mm = em.cnt["T"]
                                    ss = st_i % 4

                                    if wname == "f":
                                        fslot = 0

                                        @block.scalar
                                        def _(a):
                                            em.wait(a, "T", t_mm)
                                            stfbuf.wait_reuse(a, fslot)
                                            a.drain()
                                            a.activation(stf_g[fslot][:], pb[:], AF.Silu,
                                                         bias=biasf[:, jj:jj + 1])
                                            em.inc("A", a.drain())
                                            pbuf.use(pslot, em.tok("A"))
                                            bi = SPa.batch(a)
                                            SPa.inc(bi, a.dma_start(
                                                out=dst[s, :, jj, g * TG:(g + 1) * TG],
                                                in_=stf_g[fslot][:]))
                                            stfbuf.use(fslot, SPa.tok(bi))
                                    else:
                                        @block.scalar
                                        def _(a):
                                            em.wait(a, "T", t_mm)
                                            stbuf.wait_reuse(a, ss)
                                            a.drain()
                                            a.activation(st_g[ss][:, 0, :], pb[:], AF.Identity,
                                                         bias=bias_cols[:, jj:jj + 1])
                                            em.inc("A", a.drain())
                                        a_hi = em.cnt["A"]

                                        @block.vector
                                        def _(v):
                                            em.wait(v, "A", a_hi)
                                            v.drain()
                                            em.inc("V", v.scalar_tensor_tensor(
                                                st_g[ss][:, 1, :], pb[:], bias_cols[:, jj:jj + 1],
                                                st_g[ss][:, 0, :], ALU.add, ALU.subtract))
                                            pbuf.use(pslot, em.tok("V"))
                                        v_lo = em.cnt["V"]

                                        sslot = st_i % 4

                                        @block.scalar
                                        def _(a):
                                            em.wait(a, "V", v_lo)
                                            bi = SPa.batch(a)
                                            SPa.inc(bi, a.dma_start(
                                                out=dst[s, jj, :, :, g * TG:(g + 1) * TG]
                                                .rearrange("l p t -> p l t"),
                                                in_=st_g[ss][:]))
                                            stbuf.use(ss, SPa.tok(bi))
                                            sqbuf.wait_reuse(a, sslot)
                                            em.inc("A", a.activation(sq_g[sslot][:], st_g[ss][:, 0, :],
                                                                     AF.Square))
                                            stbuf.use(ss, em.tok("A"))
                                        pending_norm.append((em.cnt["A"], sslot,
                                                             (s * H + jj) * NG + g, normbuf))
                                    st_i += 1
                            else:
                                for tc in range(TPG):
                                    for ot in range(WH * 128 // OT):
                                        pslot = st_i % 2
                                        pb = ps_p[pslot]

                                        @block.tensor
                                        def _(t):
                                            wait_toks(t, [tok_w, tok_a])
                                            pbuf.wait_reuse(t, pslot)
                                            first = True
                                            for c in range(NCH):
                                                ah = act_g[aslot][:, c, 0, tc * 128:(tc + 1) * 128]
                                                al = act_g[aslot][:, c, 1, tc * 128:(tc + 1) * 128]
                                                wh = wt_sb[:, c, 0, ot * OT:(ot + 1) * OT]
                                                wl = wt_sb[:, c, 1, ot * OT:(ot + 1) * OT]
                                                t.matmul(pb[:, 0:OT], ah, wh, start=first, stop=False)
                                                t.matmul(pb[:, 0:OT], al, wh, start=False, stop=False)
                                                mi = t.matmul(pb[:, 0:OT], ah, wl, start=False,
                                                              stop=(c == NCH - 1))
                                                first = False
                                            em.inc("T", mi)
                                            abuf.use(aslot, em.tok("T"))
                                            wbuf.use(wslot, em.tok("T"))
                                        t_mm = em.cnt["T"]
                                        vslot = 0
                                        ss = st_i % 4

                                        @block.vector
                                        def _(v):
                                            em.wait(v, "T", t_mm)
                                            stvbuf.wait_reuse(v, vslot)
                                            stbuf.wait_reuse(v, ss)
                                            v.drain()
                                            obase = half * WH * 128 + ot * OT
                                            v.tensor_tensor(stv_g[vslot][:, 0:OT], pb[:, 0:OT],
                                                            vbb[:, obase:obase + OT], ALU.add)
                                            v.drain()
                                            v.tensor_copy(st_g[ss][:, 0, 0:OT], stv_g[vslot][:, 0:OT])
                                            v.drain()
                                            em.inc("V", v.tensor_tensor(
                                                st_g[ss][:, 1, 0:OT], stv_g[vslot][:, 0:OT],
                                                st_g[ss][:, 0, 0:OT], ALU.subtract))
                                            pbuf.use(pslot, em.tok("V"))
                                            stvbuf.use(vslot, em.tok("V"))
                                        v_ev = em.cnt["V"]

                                        @block.gpsimd
                                        def _(gp):
                                            em.wait(gp, "V", v_ev)
                                            a_blk = (g * TG + tc * 128) // 128
                                            bi = SPg.batch(gp)
                                            for hh in range(OT // HD):
                                                h = (half * WH * 128 + ot * OT) // HD + hh
                                                SPg.inc(bi, gp.dma_start(
                                                    out=sc_V[s, h, :, a_blk, :]
                                                    .rearrange("p (l d) -> p l d", l=2),
                                                    in_=st_g[ss][:, :, hh * HD:(hh + 1) * HD]))
                                            stbuf.use(ss, SPg.tok(bi))
                                        st_i += 1
            emit_pending_norms(drainall=True)

        phase_barrier()

        with contextlib.ExitStack() as pp:
            q_sb = [sb(f"p2_q{i}", [128, 2, TG], BF16, stack=pp) for i in range(2)]
            k_sb = [sb(f"p2_k{i}", [128, 2, S], BF16, stack=pp) for i in range(2)]
            v_sb = [sb(f"p2_v{i}", [128, NKT, 2 * HD], BF16, stack=pp) for i in range(2)]
            atf = [sb(f"p2_atf{i}", [128, KG * TG], F32, stack=pp) for i in range(4)]
            atp = [sb(f"p2_atp{i}", [128, 2, KG * TG], BF16, stack=pp) for i in range(4)]
            avtf = sb("p2_avtf", [128, TG], F32, stack=pp)
            avu = [sb(f"p2_avu{i}", [128, TG], F32, stack=pp) for i in range(2)]
            avt_sb = sb("p2_avt", [128, H, 2, TG], BF16, stack=pp)
            g_sb = [sb(f"p2_g{i}", [128, TG], F32, stack=pp) for i in range(2)]
            xo_sb = sb("p2_xo", [128, TPG, D], stack=pp)
            ow_sb = [sb(f"p2_ow{i}", [128, NCH, 2, 128], BF16, stack=pp) for i in range(2)]
            p_st = [sb(f"p2_pst{i}", [128, TG], F32, stack=pp) for i in range(2)]
            rc_row = sb("p2_rc", [1, TG], F32, stack=pp)
            rc_b = sb("p2_rcb", [128, TG], F32, stack=pp)
            mask01 = sb("mask01_s", [128, MW], stack=pp)
            NSLOT = 2
            ps_sc = pp.enter_context(nc.psum_tensor("p2_sc", [128, NSLOT * KG * TG], F32))
            ps_av2 = [pp.enter_context(nc.psum_tensor(f"p2_av{i}", [128, TG], F32)) for i in range(2)]
            ps_sum2 = [pp.enter_context(nc.psum_tensor(f"p2_sum{i}", [1, TG], F32)) for i in range(2)]
            negc_bb = sb("p2_negbb", [128, NSEQ * H], stack=pp)
            kvbuf = Buf(2)
            qbuf = Buf(2)
            aubuf = Buf(2)
            rcrbuf = Buf(1)
            atfbuf = Buf(4)
            atpbuf = Buf(4)
            owbuf = Buf(2)
            pstbuf = Buf(2)
            gbuf = Buf(2)
            xgbuf = Buf(1)

            for s in range(NSEQ):
                for h in range(H):
                    idx = s * H + h

                    @block.vector
                    def _(v):
                        em.wait(v, "A")
                        v.drain()
                        base = idx * NG
                        v.tensor_reduce(spool[:, 1:2], normq[:, base:base + NG], axis=AX.X, op=ALU.max)
                        v.tensor_reduce(spool[:, 2:3], normk[:, base:base + NG], axis=AX.X, op=ALU.max)
                        v.drain()
                        em.inc("V", v.tensor_tensor(spool[:, 3:4], spool[:, 1:2],
                                                    spool[:, 2:3], ALU.mult))
                    vv = em.cnt["V"]

                    @block.scalar
                    def _(a):
                        em.wait(a, "V", vv)
                        a.drain()
                        em.inc("A", a.activation(spool[:, 4:5], spool[:, 3:4], AF.Sqrt))
                    aa = em.cnt["A"]

                    @block.vector
                    def _(v):
                        em.wait(v, "A", aa)
                        v.drain()
                        em.inc("V", v.tensor_scalar(negC[:, idx:idx + 1], spool[:, 4:5],
                                                    -cfg.SCALE, None, ALU.mult))

            @block.scalar
            def _(a):
                em.wait(a, "V")
                bi = SPa.batch(a)
                SPa.inc(bi, a.dma_start(out=sc_negc[:], in_=negC[:]))
                cell["t"] = SPa.tok(bi)
            tok_ncst = cell["t"]

            @block.sync
            def _(sync):
                wait_toks(sync, [tok_ncst])
                bi = LP.batch(sync)
                LP.inc(bi, sync.dma_start(out=negc_bb[:],
                                          in_=sc_negc.partition_broadcast(128)
                                          .rearrange("p a t -> p (a t)")))
                cell["t"] = LP.tok(bi)
            tok_ncbb = cell["t"]

            @block.sync
            def _(sync):
                bi = LP.batch(sync)
                LP.inc(bi, sync.dma_start(out=mask01[:], in_=mask_h[:]))
                cell["t"] = LP.tok(bi)
            tok_mask = cell["t"]

            @block.vector
            def _(v):
                wait_toks(v, [tok_mask])
                em.inc("V", v.drain())

            epfree = [None]
            for s in range(NSEQ):
                for m in range(NMT):
                    block.flush()
                    nkt = (m + 1) * TG // 128
                    ngrp = (nkt + KG - 1) // KG

                    @block.sync
                    def _(sync):
                        xgbuf.wait_reuse(sync, 0)
                        bi = LP.batch(sync)
                        LP.inc(bi, sync.dma_start(
                            out=xo_sb[:], in_=x_h[s, m * TG:(m + 1) * TG, :]
                            .rearrange("(a p) d -> p a d", p=128)))
                        cell["t"] = LP.tok(bi)
                    tok_xg = cell["t"]

                    def load_kv(h):
                        kvslot = h % 2

                        @block.sync
                        def _(sync):
                            kvbuf.wait_reuse(sync, kvslot)
                            qbuf.wait_reuse(sync, kvslot)
                            bi = LP.batch(sync)
                            LP.inc(bi, sync.dma_start(
                                out=q_sb[kvslot][:], in_=sc_QT[s, h, :, :, m * TG:(m + 1) * TG]
                                .rearrange("l p t -> p l t")))
                            LP.inc(bi, sync.dma_start(
                                out=k_sb[kvslot][:, :, 0:nkt * 128],
                                in_=sc_KT[s, h, :, :, 0:nkt * 128]
                                .rearrange("l p t -> p l t")))
                            LP.inc(bi, sync.dma_start(
                                out=v_sb[kvslot][:, 0:nkt, :],
                                in_=sc_V[s, h, :, 0:nkt, :]))
                            cell["t"] = LP.tok(bi)
                        return cell["t"]

                    def emit_scores(h, kg, gi, tok_kv):
                        kvslot = h % 2
                        j0 = kg * KG
                        j1 = min(nkt, j0 + KG)
                        gslot = gi % 2
                        base = gslot * KG * TG

                        @block.tensor
                        def _(t):
                            wait_toks(t, [tok_kv])
                            if gi < 2 and epfree[0] is not None:
                                wait_toks(t, [epfree[0]])
                            if exp_toks[gslot] is not None:
                                wait_toks(t, [exp_toks[gslot]])
                            if kg == 0:
                                if avfree[h % 2] is not None:
                                    wait_toks(t, [avfree[h % 2], sumfree[h % 2]])
                            for j in range(j0, j1):
                                off = base + (j - j0) * TG
                                kh = k_sb[kvslot][:, 0, j * 128:(j + 1) * 128]
                                kl = k_sb[kvslot][:, 1, j * 128:(j + 1) * 128]
                                t.matmul(ps_sc[:, off:off + TG], kh, q_sb[kvslot][:, 0, :],
                                         start=True, stop=False)
                                t.matmul(ps_sc[:, off:off + TG], kh, q_sb[kvslot][:, 1, :],
                                         start=False, stop=False)
                                em.inc("T", t.matmul(ps_sc[:, off:off + TG], kl,
                                                     q_sb[kvslot][:, 0, :],
                                                     start=False, stop=True))
                        return em.tok("T")

                    def emit_exp(h, kg, gi, t_sc):
                        j0 = kg * KG
                        j1 = min(nkt, j0 + KG)
                        w = j1 - j0
                        gslot = gi % 2
                        aslot = gi % 4
                        base = gslot * KG * TG

                        @block.scalar
                        def _(a):
                            wait_toks(a, [t_sc, tok_ncbb])
                            atfbuf.wait_reuse(a, aslot)
                            a.drain()
                            em.inc("A", a.activation(atf[aslot][:, 0:w * TG],
                                                     ps_sc[:, base:base + w * TG], AF.Exp,
                                                     scale=cfg.SCALE,
                                                     bias=negc_bb[:, s * H + h:s * H + h + 1]))
                        a_exp = em.cnt["A"]
                        exp_toks[gslot] = (em.sems["A"], a_exp)

                        masked = [j for j in range(j0, j1) if j - m * TPG >= 0]
                        g_mask = None
                        if masked:
                            @block.gpsimd
                            def _(gp):
                                em.wait(gp, "A", a_exp)
                                wait_toks(gp, [tok_mask])
                                for j in masked:
                                    jj = j - m * TPG
                                    off = (TG - 128) - 128 * jj
                                    fo = (j - j0) * TG
                                    mi = gp.tensor_tensor(atf[aslot][:, fo:fo + TG],
                                                          atf[aslot][:, fo:fo + TG],
                                                          mask01[:, off:off + TG], ALU.mult)
                                em.inc("G", mi)
                            g_mask = em.cnt["G"]

                        @block.scalar
                        def _(a):
                            if g_mask is not None:
                                em.wait(a, "G", g_mask)
                            atpbuf.wait_reuse(a, aslot)
                            a.drain()
                            em.inc("A", a.activation(atp[aslot][:, 0, 0:w * TG],
                                                     atf[aslot][:, 0:w * TG], AF.Identity))
                        a_hi = em.cnt["A"]

                        @block.vector
                        def _(v):
                            em.wait(v, "A", a_hi)
                            v.drain()
                            em.inc("V", v.tensor_tensor(atp[aslot][:, 1, 0:w * TG],
                                                        atf[aslot][:, 0:w * TG],
                                                        atp[aslot][:, 0, 0:w * TG],
                                                        ALU.subtract))
                            atfbuf.use(aslot, em.tok("V"))
                        return em.tok("V")

                    def emit_av(h, kg, gi, split_tok):
                        kvslot = h % 2
                        j0 = kg * KG
                        j1 = min(nkt, j0 + KG)
                        aslot = gi % 4
                        pav = ps_av2[h % 2]
                        psum_ = ps_sum2[h % 2]

                        @block.tensor
                        def _(t):
                            wait_toks(t, [split_tok])
                            for j in range(j0, j1):
                                fo = (j - j0) * TG
                                ah = atp[aslot][:, 0, fo:fo + TG]
                                al = atp[aslot][:, 1, fo:fo + TG]
                                vh = v_sb[kvslot][:, j, 0:HD]
                                vl = v_sb[kvslot][:, j, HD:2 * HD]
                                first = (j == 0)
                                last = (j == nkt - 1)
                                t.matmul(psum_[:], onesb[:], ah,
                                         start=first, stop=False, skip_group_check=True)
                                em.inc("T", t.matmul(psum_[:], onesb[:], al,
                                                     start=False, stop=last,
                                                     skip_group_check=True))
                                t.matmul(pav[:], vh, ah, start=first, stop=False,
                                         skip_group_check=True)
                                t.matmul(pav[:], vh, al, start=False, stop=False,
                                         skip_group_check=True)
                                em.inc("T", t.matmul(pav[:], vl, ah, start=False, stop=last,
                                                     skip_group_check=True))
                            atpbuf.use(aslot, em.tok("T"))
                            kvbuf.use(kvslot, em.tok("T"))
                            qbuf.use(kvslot, em.tok("T"))
                        return em.tok("T")

                    def head_tail(h, t_av):
                        auslot = h % 2
                        pav = ps_av2[h % 2]
                        psum_ = ps_sum2[h % 2]

                        @block.scalar
                        def _(a):
                            wait_toks(a, [t_av])
                            aubuf.wait_reuse(a, auslot)
                            a.drain()
                            em.inc("A", a.activation(avu[auslot][:], pav[:], AF.Identity))
                        avfree[h % 2] = em.tok("A")

                        @block.vector
                        def _(v):
                            wait_toks(v, [t_av])
                            rcrbuf.wait_reuse(v, 0)
                            v.drain()
                            em.inc("V", v.reciprocal(rc_row[:], psum_[:]))
                        v_rc = em.cnt["V"]
                        sumfree[h % 2] = em.tok("V")

                        @block.gpsimd
                        def _(gp):
                            em.wait(gp, "V", v_rc)
                            bi = SPg.batch(gp)
                            SPg.inc(bi, gp.dma_start(out=sc_rc[s, m, h, :].unsqueeze(0),
                                                     in_=rc_row[:]))
                            cell["t"] = SPg.tok(bi)
                        tok_rcst = cell["t"]
                        rcrbuf.use(0, tok_rcst)

                        @block.gpsimd
                        def _(gp):
                            wait_toks(gp, [tok_rcst])
                            bi = SPg.batch(gp)
                            SPg.inc(bi, gp.dma_start(
                                out=rc_b[:], in_=sc_rc[s, m, h, :]
                                .unsqueeze(0).partition_broadcast(128)
                                .rearrange("p a t -> p (a t)")))
                            cell["t"] = SPg.tok(bi)
                        tok_rcb = cell["t"]

                        @block.gpsimd
                        def _(gp):
                            wait_toks(gp, [tok_rcb, avfree[h % 2]])
                            gp.tensor_tensor(avtf[:], avu[auslot][:], rc_b[:], ALU.mult)
                            gp.drain()
                            em.inc("G", gp.tensor_copy(avt_sb[:, h, 0, :], avtf[:]))
                            aubuf.use(auslot, em.tok("G"))
                            gp.drain()
                            em.inc("G", gp.tensor_tensor(avt_sb[:, h, 1, :], avtf[:],
                                                         avt_sb[:, h, 0, :], ALU.subtract))

                    exp_toks = [None, None]
                    avfree = [None, None]
                    sumfree = [None, None]
                    pend = []
                    gi = 0

                    def pop_one():
                        hh, kk, gg_, sp = pend.pop(0)
                        t_av = emit_av(hh, kk, gg_, sp)
                        if kk == ngrp - 1:
                            head_tail(hh, t_av)

                    for h in range(H):
                        while pend and pend[0][0] <= h - 2:
                            pop_one()
                        tok_kv = load_kv(h)
                        for kg in range(ngrp):
                            t_sc = emit_scores(h, kg, gi, tok_kv)
                            if len(pend) >= 3:
                                pop_one()
                            pend.append((h, kg, gi, emit_exp(h, kg, gi, t_sc)))
                            gi += 1
                    while pend:
                        pop_one()
                    v_avt = em.cnt["V"]
                    g_avt = em.cnt["G"]
                    block.flush()

                    for j in range(NCH):
                        owslot = j % 2
                        gslot = j % 2

                        @block.sync
                        def _(sync):
                            owbuf.wait_reuse(sync, owslot)
                            gbuf.wait_reuse(sync, gslot)
                            bi = LP.batch(sync)
                            LP.inc(bi, sync.dma_start(out=ow_sb[owslot][:], in_=sc_oT[j]))
                            LP.inc(bi, sync.dma_start(
                                out=g_sb[gslot][:],
                                in_=sc_g[s, :, j, m * TG:(m + 1) * TG]))
                            cell["t"] = LP.tok(bi)
                        tok_ow = cell["t"]
                        pob = ps_sc[:, (j % 2) * TG:(j % 2 + 1) * TG]

                        @block.tensor
                        def _(t):
                            wait_toks(t, [tok_ow])
                            em.wait(t, "V", v_avt)
                            em.wait(t, "G", g_avt)
                            em.wait(t, "V")
                            first = True
                            for c in range(NCH):
                                oh = ow_sb[owslot][:, c, 0, :]
                                ol = ow_sb[owslot][:, c, 1, :]
                                ah = avt_sb[:, c, 0, :]
                                al = avt_sb[:, c, 1, :]
                                t.matmul(pob, oh, ah, start=first, stop=False)
                                t.matmul(pob, oh, al, start=False, stop=False)
                                mi = t.matmul(pob, ol, ah, start=False, stop=(c == NCH - 1))
                                first = False
                            em.inc("T", mi)
                            owbuf.use(owslot, em.tok("T"))
                        t_o = em.cnt["T"]
                        pslot = j % 2

                        @block.vector
                        def _(v):
                            em.wait(v, "T", t_o)
                            wait_toks(v, [tok_xg])
                            pstbuf.wait_reuse(v, pslot)
                            v.drain()
                            em.inc("V", v.scalar_tensor_tensor(p_st[pslot][:], pob,
                                                               ob[:, j:j + 1], g_sb[gslot][:],
                                                               ALU.add, ALU.mult))
                            xgbuf.use(0, em.tok("V"))
                            gbuf.use(gslot, em.tok("V"))
                        v_p = em.cnt["V"]

                        @block.tensor
                        def _(t):
                            em.wait(t, "V", v_p)
                            for tt in range(TPG):
                                t.transpose(ps_sc[:, 2 * TG + tt * 128:2 * TG + (tt + 1) * 128],
                                            p_st[pslot][:, tt * 128:(tt + 1) * 128], ident[:])
                            em.inc("T", t.drain())
                            pstbuf.use(pslot, em.tok("T"))
                        t_tp = em.cnt["T"]

                        @block.vector
                        def _(v):
                            em.wait(v, "T", t_tp)
                            v.drain()
                            em.inc("V", v.tensor_tensor(
                                xo_sb[:, :, j * 128:(j + 1) * 128],
                                ps_sc[:, 2 * TG:3 * TG].rearrange("p (a f) -> p a f", a=TPG),
                                xo_sb[:, :, j * 128:(j + 1) * 128],
                                ALU.add))
                            xgbuf.use(0, em.tok("V"))
                    v_out = em.cnt["V"]
                    epfree[0] = em.tok("V")

                    @block.gpsimd
                    def _(gp):
                        em.wait(gp, "V", v_out)
                        bi = OP.batch(gp)
                        OP.inc(bi, gp.dma_start(
                            out=y_h[s, m * TG:(m + 1) * TG, :]
                            .rearrange("(a p) d -> p a d", p=128),
                            in_=xo_sb[:]))
                        xgbuf.use(0, OP.tok(bi))

            @block.sync
            def _(sync):
                OP.wait_all(sync)
            block.flush()

    return nc



_CACHE = {}


def _mask01(TG):
    MW = 2 * TG - 128
    p = np.arange(128)[:, None]
    c = np.arange(MW)[None, :]
    return (c - (TG - 128) >= p).astype(np.float32)


def _chunk_cols(v, NCH):
    return np.ascontiguousarray(np.asarray(v, np.float32).reshape(NCH, 128).T)


def make_in_maps(cfg, x, q_w, q_b, k_w, k_b, v_w, v_b, o_w, o_b, W, ln_w, ln_b,
                 ncores):
    NCH = cfg.NCH
    shared = {
        "qT": np.ascontiguousarray(np.asarray(q_w, np.float32).T),
        "kT": np.ascontiguousarray(np.asarray(k_w, np.float32).T),
        "vT": np.ascontiguousarray(np.asarray(v_w, np.float32).T),
        "oT": np.ascontiguousarray(np.asarray(o_w, np.float32).T),
        "W": np.ascontiguousarray(np.asarray(W, np.float32)),
        "qb": _chunk_cols(q_b, NCH),
        "kb": _chunk_cols(k_b, NCH),
        "ob": _chunk_cols(o_b, NCH),
        "vbrow": np.ascontiguousarray(np.asarray(v_b, np.float32).reshape(1, -1)),
        "lnw": _chunk_cols(ln_w, NCH),
        "lnb": _chunk_cols(ln_b, NCH),
        "id128": np.eye(128, dtype=np.float32),
        "mask01": _mask01(cfg.TG),
    }
    x = np.ascontiguousarray(np.asarray(x, np.float32))
    per = x.shape[0] // ncores
    return [dict(shared, x=np.ascontiguousarray(x[i * per:(i + 1) * per]))
            for i in range(ncores)]


def kernel(L0_attn_x, L0_attn_q_w, L0_attn_q_b, L0_attn_k_w, L0_attn_k_b,
           L0_attn_v_w, L0_attn_v_b, L0_attn_o_w, L0_attn_o_b,
           L0_ffn_W, L0_ffn_ln_w, L0_ffn_ln_b):
    cfg = Cfg()
    if "nc" not in _CACHE:
        _CACHE["nc"] = build(cfg)
    in_maps = make_in_maps(cfg, L0_attn_x, L0_attn_q_w, L0_attn_q_b, L0_attn_k_w,
                           L0_attn_k_b, L0_attn_v_w, L0_attn_v_b, L0_attn_o_w,
                           L0_attn_o_b, L0_ffn_W, L0_ffn_ln_w, L0_ffn_ln_b, NCORES)
    res = run_bass_kernel_spmd(_CACHE["nc"], in_maps, core_ids=list(range(NCORES)))
    return np.concatenate([np.asarray(r["y"], np.float32) for r in res.results], axis=0)


# version 22
# speedup vs baseline: 1.0352x; 1.0352x over previous
import math
import contextlib
import numpy as np

import concourse.bass as bass
import concourse.mybir as mybir
from concourse.bass_utils import run_bass_kernel_spmd

F32 = mybir.dt.float32
BF16 = mybir.dt.bfloat16
AF = mybir.ActivationFunctionType
ALU = mybir.AluOpType
AX = mybir.AxisListType

EPS = 1e-5
NCORES = 8


class Cfg:
    def __init__(self, S=2048, D=2048, H=16, NSEQ=2, WH=None):
        self.S, self.D, self.H, self.NSEQ = S, D, H, NSEQ
        self.HD = D // H
        assert self.HD == 128
        self.SCALE = 1.0 / math.sqrt(self.HD)
        self.NCH = D // 128
        self.TG = 512
        assert S % self.TG == 0
        self.NG = S // self.TG
        self.TPG = self.TG // 128
        self.WH = min(8, self.NCH) if WH is None else WH
        self.NKT = S // 128


class Emit:

    def __init__(self, nc, es):
        self.sems = {}
        self.cnt = {}
        for n in ["V", "A", "T", "G"]:
            self.sems[n] = es.enter_context(nc.semaphore("sem_" + n))
            self.cnt[n] = 0

    def inc(self, name, instr, amt=1):
        instr.then_inc(self.sems[name], amt)
        self.cnt[name] += amt
        return instr

    def tok(self, name):
        return (self.sems[name], self.cnt[name])

    def wait(self, eng, name, val=None):
        if val is None:
            val = self.cnt[name]
        if val > 0:
            eng.wait_ge(self.sems[name], val)


def wait_toks(eng, toks):
    best = {}
    for sem, val in toks:
        key = id(sem)
        if key not in best or best[key][1] < val:
            best[key] = (sem, val)
    for sem, val in best.values():
        eng.wait_ge(sem, val)


class Pool:

    def __init__(self, nc, es, prefix, n):
        self.sems = [es.enter_context(nc.semaphore(f"{prefix}{i}")) for i in range(n)]
        self.cnt = [0] * n
        self.i = 0

    def batch(self, eng):
        idx = self.i % len(self.sems)
        self.i += 1
        if self.cnt[idx] > 0:
            eng.wait_ge(self.sems[idx], self.cnt[idx])
        return idx

    def inc(self, idx, instr):
        instr.then_inc(self.sems[idx], 16)
        self.cnt[idx] += 16
        return instr

    def tok(self, idx):
        return (self.sems[idx], self.cnt[idx])

    def wait_all(self, eng):
        for j, s in enumerate(self.sems):
            if self.cnt[j] > 0:
                eng.wait_ge(s, self.cnt[j])


class Buf:

    def __init__(self, nslots):
        self.n = nslots
        self.marks = [[] for _ in range(nslots)]

    def use(self, slot, tok):
        self.marks[slot % self.n].append(tok)

    def wait_reuse(self, eng, slot):
        wait_toks(eng, self.marks[slot % self.n])
        self.marks[slot % self.n] = []


class _RecInstr:
    __slots__ = ("incs",)

    def __init__(self):
        self.incs = []

    def then_inc(self, sem, amt):
        self.incs.append((sem, amt))
        return self


class _Rec:

    def __init__(self):
        self.calls = []

    def __getattr__(self, name):
        def call(*a, **k):
            ri = _RecInstr()
            self.calls.append((name, a, k, ri))
            return ri
        return call

    def replay(self, eng):
        for name, a, k, ri in self.calls:
            instr = getattr(eng, name)(*a, **k)
            for sem, amt in ri.incs:
                instr.then_inc(sem, amt)


class QBlock:

    NAMES = ("sync", "vector", "scalar", "tensor", "gpsimd")

    def __init__(self, block):
        self.block = block
        self.q = {n: [] for n in self.NAMES}

    def _deco(self, n):
        def deco(fn):
            rec = _Rec()
            fn(rec)
            self.q[n].append(rec)
        return deco

    @property
    def sync(self):
        return self._deco("sync")

    @property
    def vector(self):
        return self._deco("vector")

    @property
    def scalar(self):
        return self._deco("scalar")

    @property
    def tensor(self):
        return self._deco("tensor")

    @property
    def gpsimd(self):
        return self._deco("gpsimd")

    def flush(self):
        for n in self.NAMES:
            recs = self.q[n]
            if recs:
                def runner(eng, recs=list(recs)):
                    for rec in recs:
                        rec.replay(eng)
                getattr(self.block, n)(runner)
                self.q[n] = []


def build(cfg: Cfg):
    S, D, H, NSEQ = cfg.S, cfg.D, cfg.H, cfg.NSEQ
    NCH, TG, NG, TPG, WH, NKT = cfg.NCH, cfg.TG, cfg.NG, cfg.TPG, cfg.WH, cfg.NKT
    HD = cfg.HD
    MW = 2 * TG - 128
    NMT = S // TG
    NHALF = NCH // WH
    KG = 2

    nc = bass.Bass("TRN2", target_bir_lowering=False)

    x_h = nc.dram_tensor("x", [NSEQ, S, D], F32, kind="ExternalInput").ap()
    qT_h = nc.dram_tensor("qT", [D, D], F32, kind="ExternalInput").ap()
    kT_h = nc.dram_tensor("kT", [D, D], F32, kind="ExternalInput").ap()
    vT_h = nc.dram_tensor("vT", [D, D], F32, kind="ExternalInput").ap()
    oT_h = nc.dram_tensor("oT", [D, D], F32, kind="ExternalInput").ap()
    W_h = nc.dram_tensor("W", [D, D], F32, kind="ExternalInput").ap()
    qb_h = nc.dram_tensor("qb", [128, NCH], F32, kind="ExternalInput").ap()
    kb_h = nc.dram_tensor("kb", [128, NCH], F32, kind="ExternalInput").ap()
    ob_h = nc.dram_tensor("ob", [128, NCH], F32, kind="ExternalInput").ap()
    vbrow_h = nc.dram_tensor("vbrow", [1, D], F32, kind="ExternalInput").ap()
    lnw_h = nc.dram_tensor("lnw", [128, NCH], F32, kind="ExternalInput").ap()
    lnb_h = nc.dram_tensor("lnb", [128, NCH], F32, kind="ExternalInput").ap()
    id_h = nc.dram_tensor("id128", [128, 128], F32, kind="ExternalInput").ap()
    mask_h = nc.dram_tensor("mask01", [128, MW], F32, kind="ExternalInput").ap()
    y_h = nc.dram_tensor("y", [NSEQ, S, D], F32, kind="ExternalOutput").ap()

    sc_xnT = nc.dram_tensor("sc_xnT", [NSEQ, NG, 128, NCH, 2, TG], BF16).ap()
    sc_uT = nc.dram_tensor("sc_uT", [NSEQ, NG, 128, NCH, 2, TG], BF16).ap()
    sc_QT = nc.dram_tensor("sc_QT", [NSEQ, H, 2, HD, S], BF16).ap()
    sc_KT = nc.dram_tensor("sc_KT", [NSEQ, H, 2, HD, S], BF16).ap()
    sc_V = nc.dram_tensor("sc_V", [NSEQ, H, 128, NKT, 2 * HD], BF16).ap()
    sc_g = nc.dram_tensor("sc_g", [NSEQ, 128, NCH, S], F32).ap()
    sc_Wq = nc.dram_tensor("sc_Wq", [NCH, 128, D], BF16).ap()
    sc_w = {w: nc.dram_tensor(f"sc_w{w}", [NHALF, 128, NCH, 2, WH * 128], BF16).ap()
            for w in ["q", "k", "v"]}
    sc_oT = nc.dram_tensor("sc_oT", [NCH, 128, NCH, 2, 128], BF16).ap()
    sc_negc = nc.dram_tensor("sc_negc", [1, NSEQ * H], F32).ap()
    sc_rc = nc.dram_tensor("sc_rc", [NSEQ, S // 512, H, 512], F32).ap()

    es = contextlib.ExitStack()
    with es:
        em = Emit(nc, es)
        LP = Pool(nc, es, "lp", 20)
        SPa = Pool(nc, es, "spa", 12)
        SPg = Pool(nc, es, "spg", 8)
        OP = Pool(nc, es, "op", 4)
        block = QBlock(es.enter_context(nc.Block()))

        def sb(name, shape, dt=F32, stack=es):
            return stack.enter_context(nc.sbuf_tensor(name, shape, dt))

        ident = sb("ident_s", [128, 128])
        qb = sb("qb_s", [128, NCH])
        kb = sb("kb_s", [128, NCH])
        ob = sb("ob_s", [128, NCH])
        lnw = sb("lnw_s", [128, NCH])
        lnb = sb("lnb_s", [128, NCH])
        lnb_p = sb("lnbp_s", [128, 2, NCH], BF16)
        biasf = sb("biasf_s", [128, NCH])
        vbb = sb("vbb_s", [128, D])
        vbrow = sb("vbrow_s", [1, D])
        onesf = sb("onesf_s", [128, 1])
        onesb = sb("onesb_s", [128, 1], BF16)
        ones_row = sb("onesrow_s", [1, 128])
        normq = sb("normq_s", [1, NSEQ * H * NG])
        normk = sb("normk_s", [1, NSEQ * H * NG])
        vpool_ = sb("vpool_s", [128, 96])
        spool = sb("spool_s", [1, 8])
        negC = sb("negC_s", [1, NSEQ * H])

        cell = {}

        @block.sync
        def _(sync):
            bi = LP.batch(sync)
            for dst, srch in [(ident, id_h), (qb, qb_h),
                              (kb, kb_h), (ob, ob_h), (lnw, lnw_h), (lnb, lnb_h),
                              (vbrow, vbrow_h)]:
                LP.inc(bi, sync.dma_start(out=dst[:], in_=srch[:]))
            cell["t"] = LP.tok(bi)
        tok_misc = cell["t"]

        @block.vector
        def _(v):
            v.memset(onesf[:], 1.0)
            v.memset(ones_row[:], 1.0)
            v.memset(onesb[:], 1.0)
            wait_toks(v, [tok_misc])
            v.drain()
            v.tensor_copy(lnb_p[:, 0, :], lnb[:])
            v.drain()
            em.inc("V", v.tensor_tensor(lnb_p[:, 1, :], lnb[:], lnb_p[:, 0, :], ALU.subtract))
        V_init = em.cnt["V"]

        def phase_barrier():
            block.flush()

            @block.vector
            def _(v):
                em.inc("V", v.drain())

            @block.scalar
            def _(a):
                em.inc("A", a.drain())

            @block.tensor
            def _(t):
                em.inc("T", t.drain())

            @block.gpsimd
            def _(gp):
                em.inc("G", gp.drain())
            marks = {n: em.cnt[n] for n in ["V", "A", "T", "G"]}

            def waits(e, skip=None):
                for n, vv in marks.items():
                    if n != skip:
                        em.wait(e, n, vv)
                LP.wait_all(e)
                SPa.wait_all(e)
                SPg.wait_all(e)
                OP.wait_all(e)

            @block.vector
            def _(v):
                waits(v, "V")

            @block.scalar
            def _(a):
                waits(a, "A")

            @block.tensor
            def _(t):
                waits(t, "T")

            @block.gpsimd
            def _(gp):
                waits(gp, "G")

            @block.sync
            def _(sync):
                waits(sync)
            block.flush()

        wq_store_toks = []
        absacc = sb("w0_acc", [128, 2])
        with contextlib.ExitStack() as pp:
            x_g = [sb(f"p0_x{i}", [128, TPG // 2, D], stack=pp) for i in range(2)]
            xn_g = [sb(f"p0_xn{i}", [128, TPG // 2, D], stack=pp) for i in range(2)]
            v_g = [sb(f"p0_v{i}", [128, TPG // 2, D], stack=pp) for i in range(2)]
            ev = [sb(f"p0_ev{i}", [128, 2, TG // 2], BF16, stack=pp) for i in range(4)]
            ps_t = [pp.enter_context(nc.psum_tensor(f"p0_pt{i}", [128, TG], F32)) for i in range(2)]
            xbuf = Buf(2)
            evbuf = Buf(4)
            bankbuf = Buf(2)
            w_tile = [sb(f"w0_t{i}", [128, D], stack=pp) for i in range(2)]
            w_hi = [sb(f"w0_hi{i}", [128, D], BF16, stack=pp) for i in range(2)]
            w_lo = [sb(f"w0_lo{i}", [128, D], BF16, stack=pp) for i in range(2)]
            ps_b512 = pp.enter_context(nc.psum_tensor("ps_b512", [128, 512], F32))
            wbuf = Buf(2)
            hlbuf = Buf(2)

            for i in range((D + 511) // 512):
                n = min(512, D - i * 512)

                @block.tensor
                def _(t):
                    wait_toks(t, [tok_misc])
                    em.wait(t, "V")
                    em.inc("T", t.matmul(ps_b512[:, 0:n], ones_row[:],
                                         vbrow[:, i * 512:i * 512 + n],
                                         start=True, stop=True))
                t_bc = em.cnt["T"]

                @block.vector
                def _(v):
                    em.wait(v, "T", t_bc)
                    v.drain()
                    em.inc("V", v.tensor_copy(vbb[:, i * 512:i * 512 + n], ps_b512[:, 0:n]))

            _wjob_i = [0]

            def emit_wjob(jid):
                if jid >= 5 * NCH:
                    return
                ci = _wjob_i[0]
                _wjob_i[0] += 1
                if jid < 4 * NCH:
                    wname, wsrc = [("q", qT_h), ("k", kT_h), ("v", vT_h),
                                   ("o", oT_h)][jid // NCH]
                    c = jid % NCH
                else:
                    wname, wsrc = "abs", W_h
                    c = jid - 4 * NCH

                @block.sync
                def _(sync):
                    wbuf.wait_reuse(sync, ci)
                    bi = LP.batch(sync)
                    LP.inc(bi, sync.dma_start(out=w_tile[ci % 2][:],
                                              in_=wsrc[c * 128:(c + 1) * 128, :]))
                    cell["t"] = LP.tok(bi)
                tok_w = cell["t"]

                if wname == "abs":
                    @block.vector
                    def _(v):
                        wait_toks(v, [tok_w])
                        v.drain()
                        v.tensor_reduce(absacc[:, 1:2], w_tile[ci % 2][:], axis=AX.X,
                                        op=ALU.add, apply_absolute_value=True)
                        v.drain()
                        if c == 0:
                            em.inc("V", v.tensor_copy(absacc[:, 0:1], absacc[:, 1:2]))
                        else:
                            em.inc("V", v.tensor_tensor(absacc[:, 0:1], absacc[:, 0:1],
                                                        absacc[:, 1:2], ALU.add))
                        wbuf.use(ci, em.tok("V"))
                    return

                @block.vector
                def _(v):
                    wait_toks(v, [tok_w])
                    hlbuf.wait_reuse(v, ci)
                    v.drain()
                    v.tensor_copy(w_hi[ci % 2][:], w_tile[ci % 2][:])
                    v.drain()
                    em.inc("V", v.tensor_tensor(w_lo[ci % 2][:], w_tile[ci % 2][:],
                                                w_hi[ci % 2][:], ALU.subtract))
                    wbuf.use(ci, em.tok("V"))
                vc = em.cnt["V"]

                @block.gpsimd
                def _(gp):
                    em.wait(gp, "V", vc)
                    bi = SPg.batch(gp)
                    if wname != "o":
                        for half in range(NHALF):
                            lo_ = half * WH * 128
                            hi_ = (half + 1) * WH * 128
                            SPg.inc(bi, gp.dma_start(
                                out=sc_w[wname][half, :, c, 0, :], in_=w_hi[ci % 2][:, lo_:hi_]))
                            SPg.inc(bi, gp.dma_start(
                                out=sc_w[wname][half, :, c, 1, :], in_=w_lo[ci % 2][:, lo_:hi_]))
                    else:
                        for j in range(NCH):
                            SPg.inc(bi, gp.dma_start(
                                out=sc_oT[j, :, c, 0, :], in_=w_hi[ci % 2][:, j * 128:(j + 1) * 128]))
                            SPg.inc(bi, gp.dma_start(
                                out=sc_oT[j, :, c, 1, :], in_=w_lo[ci % 2][:, j * 128:(j + 1) * 128]))
                    hlbuf.use(ci, SPg.tok(bi))
                    wq_store_toks.append(SPg.tok(bi))
            TPS = TPG // 2
            xnvbuf = Buf(2)
            prev_V2 = [0, 0]
            NIT = NSEQ * NG * 2
            jpg = -(-5 * NCH // NIT)
            sg_list = [(s, g, half) for s in range(NSEQ) for g in range(NG) for half in range(2)]

            def stageA(it):
                s, g, half = sg_list[it]
                slot = it % 2
                vp = vpool_[:, (it % 2) * 48:(it % 2) * 48 + 48]
                toff = half * TPS * 128

                @block.sync
                def _(sync):
                    xbuf.wait_reuse(sync, slot)
                    bi = LP.batch(sync)
                    LP.inc(bi, sync.dma_start(
                        out=x_g[slot][:],
                        in_=x_h[s, g * TG + toff:g * TG + toff + TPS * 128, :]
                        .rearrange("(a p) d -> p a d", p=128)))
                    cell["t"] = LP.tok(bi)
                tok_x = cell["t"]

                @block.scalar
                def _(a):
                    wait_toks(a, [tok_x])
                    em.wait(a, "V", prev_V2[slot])
                    xnvbuf.wait_reuse(a, slot)
                    a.drain()
                    for tt in range(TPS):
                        a.activation(xn_g[slot][:, tt, :], x_g[slot][:, tt, :], AF.Square,
                                     accum_out=vp[:, tt:tt + 1])
                    em.inc("A", a.drain())
                a_sq1 = em.cnt["A"]

                @block.vector
                def _(v):
                    wait_toks(v, [tok_x])
                    v.drain()
                    for tt in range(TPS):
                        v.tensor_reduce(vp[:, 4 + tt:5 + tt], x_g[slot][:, tt, :],
                                        axis=AX.X, op=ALU.add)
                    v.drain()
                    v.tensor_scalar(vp[:, 8:8 + TPS], vp[:, 4:4 + TPS], 1.0 / D, None, ALU.mult)
                    em.wait(v, "A", a_sq1)
                    v.tensor_scalar(vp[:, 12:12 + TPS], vp[:, 0:TPS], 1.0 / D, None, ALU.mult)
                    v.drain()
                    v.tensor_tensor(vp[:, 16:16 + TPS], vp[:, 8:8 + TPS],
                                    vp[:, 8:8 + TPS], ALU.mult)
                    v.drain()
                    v.tensor_tensor(vp[:, 12:12 + TPS], vp[:, 12:12 + TPS],
                                    vp[:, 16:16 + TPS], ALU.subtract)
                    v.drain()
                    v.tensor_scalar(vp[:, 12:12 + TPS], vp[:, 12:12 + TPS], EPS, None, ALU.add)
                    v.drain()
                    em.inc("V", v.reciprocal(vp[:, 16:16 + TPS], vp[:, 12:12 + TPS]))
                v_rc1 = em.cnt["V"]

                @block.scalar
                def _(a):
                    em.wait(a, "V", v_rc1)
                    a.drain()
                    em.inc("A", a.activation(vp[:, 16:16 + TPS], vp[:, 16:16 + TPS], AF.Sqrt))
                a_r1 = em.cnt["A"]

                @block.vector
                def _(v):
                    em.wait(v, "A", a_r1)
                    v.drain()
                    em.inc("V", v.scalar_tensor_tensor(vp[:, 20:20 + TPS], vp[:, 8:8 + TPS],
                                                       -1.0, vp[:, 16:16 + TPS],
                                                       ALU.mult, ALU.mult))
                return em.cnt["V"]

            def stageB(it, v_b1):
                s, g, half = sg_list[it]
                slot = it % 2
                vp = vpool_[:, (it % 2) * 48:(it % 2) * 48 + 48]
                toff = half * TPS * 128

                @block.scalar
                def _(a):
                    em.wait(a, "V", v_b1)
                    a.drain()
                    for tt in range(TPS):
                        a.activation(xn_g[slot][:, tt, :], x_g[slot][:, tt, :], AF.Identity,
                                     scale=vp[:, 16 + tt:17 + tt], bias=vp[:, 20 + tt:21 + tt])
                    a.drain()
                    for tt in range(TPS):
                        a.activation(v_g[slot][:, tt, :], xn_g[slot][:, tt, :], AF.Square,
                                     accum_out=vp[:, 24 + tt:25 + tt])
                    em.inc("A", a.drain())
                    xbuf.use(slot, em.tok("A"))
                a_sq2 = em.cnt["A"]

                @block.vector
                def _(v):
                    em.wait(v, "A", a_sq2)
                    v.drain()
                    for tt in range(TPS):
                        v.tensor_reduce(vp[:, 28 + tt:29 + tt], xn_g[slot][:, tt, :],
                                        axis=AX.X, op=ALU.add)
                    v.drain()
                    v.tensor_scalar(vp[:, 32:32 + TPS], vp[:, 28:28 + TPS], 1.0 / D, None, ALU.mult)
                    v.tensor_scalar(vp[:, 36:36 + TPS], vp[:, 24:24 + TPS], 1.0 / D, None, ALU.mult)
                    v.drain()
                    v.tensor_tensor(vp[:, 40:40 + TPS], vp[:, 32:32 + TPS],
                                    vp[:, 32:32 + TPS], ALU.mult)
                    v.drain()
                    v.tensor_tensor(vp[:, 36:36 + TPS], vp[:, 36:36 + TPS],
                                    vp[:, 40:40 + TPS], ALU.subtract)
                    v.drain()
                    v.tensor_scalar(vp[:, 36:36 + TPS], vp[:, 36:36 + TPS], EPS, None, ALU.add)
                    v.drain()
                    em.inc("V", v.reciprocal(vp[:, 40:40 + TPS], vp[:, 36:36 + TPS]))
                v_rc2 = em.cnt["V"]

                @block.scalar
                def _(a):
                    em.wait(a, "V", v_rc2)
                    a.drain()
                    em.inc("A", a.activation(vp[:, 40:40 + TPS], vp[:, 40:40 + TPS], AF.Sqrt))
                a_r2 = em.cnt["A"]

                @block.vector
                def _(v):
                    em.wait(v, "A", a_r2)
                    v.drain()
                    em.inc("V", v.scalar_tensor_tensor(vp[:, 44:44 + TPS], vp[:, 32:32 + TPS],
                                                       -1.0, vp[:, 40:40 + TPS],
                                                       ALU.mult, ALU.mult))
                v_b2 = em.cnt["V"]
                prev_V2[slot] = v_b2

                @block.scalar
                def _(a):
                    em.wait(a, "V", v_b2)
                    a.drain()
                    for tt in range(TPS):
                        a.activation(v_g[slot][:, tt, :], xn_g[slot][:, tt, :], AF.Identity,
                                     scale=vp[:, 40 + tt:41 + tt], bias=vp[:, 44 + tt:45 + tt])
                    em.inc("A", a.drain())
                a_tiles = em.cnt["A"]

                for which, (src_g, dst_sc) in enumerate(
                        [(xn_g, sc_xnT), (v_g, sc_uT)]):
                    for c in range(NCH):
                        bidx = which * NCH + c
                        bank = ps_t[bidx % 2]

                        @block.tensor
                        def _(t):
                            em.wait(t, "A", a_tiles)
                            bankbuf.wait_reuse(t, bidx)
                            for tt in range(TPS):
                                t.transpose(bank[:, tt * 128:(tt + 1) * 128],
                                            src_g[slot][:, tt, c * 128:(c + 1) * 128],
                                            ident[:])
                            em.inc("T", t.drain())
                            xnvbuf.use(slot, em.tok("T"))
                        t_tr = em.cnt["T"]
                        eslot = bidx % 4

                        @block.scalar
                        def _(a):
                            em.wait(a, "T", t_tr)
                            evbuf.wait_reuse(a, eslot)
                            a.drain()
                            if which == 0:
                                a.activation(ev[eslot][:, 0, :], bank[:, 0:TPS * 128],
                                             AF.Identity)
                            else:
                                a.activation(ev[eslot][:, 0, :], bank[:, 0:TPS * 128],
                                             AF.Identity, scale=lnw[:, c:c + 1])
                            em.inc("A", a.drain())
                        a_hi = em.cnt["A"]

                        @block.vector
                        def _(v):
                            em.wait(v, "A", a_hi)
                            v.drain()
                            if which == 0:
                                em.inc("V", v.tensor_tensor(ev[eslot][:, 1, :],
                                                            bank[:, 0:TPS * 128],
                                                            ev[eslot][:, 0, :], ALU.subtract))
                            else:
                                em.inc("V", v.scalar_tensor_tensor(
                                    ev[eslot][:, 1, :], bank[:, 0:TPS * 128],
                                    lnw[:, c:c + 1],
                                    ev[eslot][:, 0, :], ALU.mult, ALU.subtract))
                            bankbuf.use(bidx, em.tok("V"))
                        v_lo = em.cnt["V"]

                        @block.scalar
                        def _(a):
                            em.wait(a, "V", v_lo)
                            bi = SPa.batch(a)
                            SPa.inc(bi, a.dma_start(
                                out=dst_sc[s, g, :, c, :, toff:toff + TPS * 128],
                                in_=ev[eslot][:]))
                            evbuf.use(eslot, SPa.tok(bi))
                for jid in range(it * jpg, (it + 1) * jpg):
                    emit_wjob(jid)
                block.flush()

            vb1 = stageA(0)
            for it in range(NIT):
                nxt = stageA(it + 1) if it + 1 < NIT else None
                stageB(it, vb1)
                vb1 = nxt
        phase_barrier()

        with contextlib.ExitStack() as pp:
            w_tile = [sb(f"wb_t{i}", [128, D], stack=pp) for i in range(2)]
            wq_sb = sb("wb_wq", [128, NCH, D], BF16, stack=pp)
            thr_b = sb("wb_thr", [128, 1], stack=pp)
            mneg = sb("wb_mneg", [128, D], stack=pp)
            ps_w0 = pp.enter_context(nc.psum_tensor("ps_w0", [1, 1], F32))
            ps_thr = pp.enter_context(nc.psum_tensor("ps_thr", [128, 1], F32))
            ps_bias = pp.enter_context(nc.psum_tensor("ps_bias", [128, NCH], F32))
            wbuf = Buf(2)

            @block.tensor
            def _(t):
                em.wait(t, "V")
                em.inc("T", t.matmul(ps_w0[:], absacc[:, 0:1], onesf[:], start=True, stop=True))
            T_sum = em.cnt["T"]

            @block.vector
            def _(v):
                em.wait(v, "T", T_sum)
                v.drain()
                em.inc("V", v.tensor_scalar(spool[:, 0:1], ps_w0[:], 0.5 / (D * D), None, ALU.mult))
            V_thr = em.cnt["V"]

            @block.tensor
            def _(t):
                em.wait(t, "V", V_thr)
                em.inc("T", t.matmul(ps_thr[:], ones_row[:], spool[:, 0:1],
                                     start=True, stop=True))
            T_thr = em.cnt["T"]

            @block.vector
            def _(v):
                em.wait(v, "T", T_thr)
                v.drain()
                em.inc("V", v.tensor_copy(thr_b[:], ps_thr[:]))
            V_thrb = em.cnt["V"]

            for c in range(NCH):
                @block.sync
                def _(sync):
                    wbuf.wait_reuse(sync, c)
                    bi = LP.batch(sync)
                    LP.inc(bi, sync.dma_start(out=w_tile[c % 2][:],
                                              in_=W_h[c * 128:(c + 1) * 128, :]))
                    cell["t"] = LP.tok(bi)
                tok_w = cell["t"]

                @block.vector
                def _(v):
                    wait_toks(v, [tok_w])
                    em.wait(v, "V", V_thrb)
                    v.drain()
                    v.tensor_scalar(mneg[:], w_tile[c % 2][:], thr_b[:], 0.0, ALU.add, ALU.is_lt)
                    v.drain()
                    em.inc("V", v.scalar_tensor_tensor(wq_sb[:, c, :], w_tile[c % 2][:], thr_b[:],
                                                       mneg[:], ALU.is_gt, ALU.subtract))
                    wbuf.use(c, em.tok("V"))
                vc = em.cnt["V"]

                @block.gpsimd
                def _(gp):
                    em.wait(gp, "V", vc)
                    bi = SPg.batch(gp)
                    SPg.inc(bi, gp.dma_start(out=sc_Wq[c], in_=wq_sb[:, c, :]))
                    wq_store_toks.append(SPg.tok(bi))
            V_wq = em.cnt["V"]

            @block.tensor
            def _(t):
                em.wait(t, "V", max(V_wq, V_init))
                for j in range(NCH):
                    for c in range(NCH):
                        for hl in range(2):
                            t.matmul(ps_bias[:, j:j + 1], wq_sb[:, c, j * 128:(j + 1) * 128],
                                     lnb_p[:, hl, c:c + 1],
                                     start=(c == 0 and hl == 0),
                                     stop=(c == NCH - 1 and hl == 1),
                                     skip_group_check=True)
                em.inc("T", t.drain())
            T_bias = em.cnt["T"]

            @block.vector
            def _(v):
                em.wait(v, "T", T_bias)
                v.drain()
                em.inc("V", v.tensor_copy(biasf[:], ps_bias[:]))

            @block.gpsimd
            def _(gp):
                wait_toks(gp, wq_store_toks)
                em.wait(gp, "T", T_bias)
                em.inc("G", gp.drain())

        phase_barrier()

        with contextlib.ExitStack() as pp:
            wt_sb_ = [sb(f"p1_w{i}", [128, NCH, 2, WH * 128], BF16, stack=pp) for i in range(1)]
            act_g = [sb(f"p1_a{i}", [128, NCH, 2, TG], BF16, stack=pp) for i in range(2)]
            st_g = [sb(f"p1_s{i}", [128, 2, TG], BF16, stack=pp) for i in range(4)]
            stv_g = [sb("p1_sv0", [128, TG], F32, stack=pp)]
            stf_g = [sb("p1_sf0", [128, TG], F32, stack=pp)]
            sq_g = [sb(f"p1_sq{i}", [128, TG], BF16, stack=pp) for i in range(4)]
            ps_p = [pp.enter_context(nc.psum_tensor(f"p1_p{i}", [128, TG], F32)) for i in range(2)]
            ps_n = [pp.enter_context(nc.psum_tensor(f"p1_n{i}", [1, TG], F32)) for i in range(2)]
            ps_n_buf = Buf(2)
            abuf = Buf(2)
            stbuf = Buf(4)
            stvbuf = Buf(1)
            stfbuf = Buf(1)
            wbuf = Buf(1)
            pbuf = Buf(2)

            OT = min(512, WH * 128)
            sqbuf = Buf(4)
            pending_norm = []
            nrm_i = [0]

            def emit_pending_norms(drainall=False):
                while (len(pending_norm) > 2) or (drainall and pending_norm):
                    a_ev, sslot, pos, normbuf = pending_norm.pop(0)
                    ni = nrm_i[0]
                    nrm_i[0] += 1

                    @block.tensor
                    def _(t):
                        em.wait(t, "A", a_ev)
                        ps_n_buf.wait_reuse(t, ni)
                        em.inc("T", t.matmul(ps_n[ni % 2][:], onesb[:], sq_g[sslot][:],
                                             start=True, stop=True))
                        sqbuf.use(sslot, em.tok("T"))
                    t_n = em.cnt["T"]

                    @block.vector
                    def _(v):
                        em.wait(v, "T", t_n)
                        v.drain()
                        em.inc("V", v.tensor_reduce(normbuf[:, pos:pos + 1], ps_n[ni % 2][:],
                                                    axis=AX.X, op=ALU.max))
                        ps_n_buf.use(ni, em.tok("V"))

            jobs = [("q", "q", qb, sc_QT, normq, sc_xnT),
                    ("k", "k", kb, sc_KT, normk, sc_xnT),
                    ("v", "v", None, sc_V, None, sc_xnT),
                    ("f", None, None, sc_g, None, sc_uT)]
            st_i = 0
            w_i = 0
            for wname, wkey, bias_cols, dst, normbuf, actsrc in jobs:
                for half in range(NHALF):
                    wslot = 0
                    wt_sb = wt_sb_[wslot]
                    w_i += 1

                    @block.sync
                    def _(sync):
                        wbuf.wait_reuse(sync, wslot)
                        bi = LP.batch(sync)
                        if wkey is not None:
                            LP.inc(bi, sync.dma_start(out=wt_sb[:], in_=sc_w[wkey][half]))
                        else:
                            LP.inc(bi, sync.dma_start(
                                out=wt_sb[:, :, 0, :],
                                in_=sc_Wq[:, :, half * WH * 128:(half + 1) * WH * 128]
                                .rearrange("c p o -> p c o")))
                        cell["t"] = LP.tok(bi)
                    tok_w = cell["t"]

                    for s in range(NSEQ):
                        for g in range(NG):
                            if (s * NG + g) % 2 == 0:
                                block.flush()
                            aslot = st_i % 2

                            @block.sync
                            def _(sync):
                                abuf.wait_reuse(sync, aslot)
                                bi = LP.batch(sync)
                                LP.inc(bi, sync.dma_start(
                                    out=act_g[aslot][:], in_=actsrc[s, g]))
                                cell["t"] = LP.tok(bi)
                            tok_a = cell["t"]

                            if wname != "v":
                                for j in range(WH):
                                    jj = half * WH + j
                                    pslot = st_i % 2
                                    pb = ps_p[pslot]

                                    emit_pending_norms()

                                    @block.tensor
                                    def _(t):
                                        wait_toks(t, [tok_w, tok_a])
                                        pbuf.wait_reuse(t, pslot)
                                        first = True
                                        for c in range(NCH):
                                            wh = wt_sb[:, c, 0, j * 128:(j + 1) * 128]
                                            wl = wt_sb[:, c, 1, j * 128:(j + 1) * 128]
                                            ah = act_g[aslot][:, c, 0, :]
                                            al = act_g[aslot][:, c, 1, :]
                                            if wname == "f":
                                                t.matmul(pb[:], wh, ah, start=first, stop=False)
                                                mi = t.matmul(pb[:], wh, al, start=False,
                                                              stop=(c == NCH - 1))
                                            else:
                                                t.matmul(pb[:], wh, ah, start=first, stop=False)
                                                t.matmul(pb[:], wh, al, start=False, stop=False)
                                                mi = t.matmul(pb[:], wl, ah, start=False,
                                                              stop=(c == NCH - 1))
                                            first = False
                                        em.inc("T", mi)
                                        abuf.use(aslot, em.tok("T"))
                                        wbuf.use(wslot, em.tok("T"))
                                    t_mm = em.cnt["T"]
                                    ss = st_i % 4

                                    if wname == "f":
                                        fslot = 0

                                        @block.scalar
                                        def _(a):
                                            em.wait(a, "T", t_mm)
                                            stfbuf.wait_reuse(a, fslot)
                                            a.drain()
                                            a.activation(stf_g[fslot][:], pb[:], AF.Silu,
                                                         bias=biasf[:, jj:jj + 1])
                                            em.inc("A", a.drain())
                                            pbuf.use(pslot, em.tok("A"))
                                            bi = SPa.batch(a)
                                            SPa.inc(bi, a.dma_start(
                                                out=dst[s, :, jj, g * TG:(g + 1) * TG],
                                                in_=stf_g[fslot][:]))
                                            stfbuf.use(fslot, SPa.tok(bi))
                                    else:
                                        @block.scalar
                                        def _(a):
                                            em.wait(a, "T", t_mm)
                                            stbuf.wait_reuse(a, ss)
                                            a.drain()
                                            a.activation(st_g[ss][:, 0, :], pb[:], AF.Identity,
                                                         bias=bias_cols[:, jj:jj + 1])
                                            em.inc("A", a.drain())
                                        a_hi = em.cnt["A"]

                                        @block.vector
                                        def _(v):
                                            em.wait(v, "A", a_hi)
                                            v.drain()
                                            em.inc("V", v.scalar_tensor_tensor(
                                                st_g[ss][:, 1, :], pb[:], bias_cols[:, jj:jj + 1],
                                                st_g[ss][:, 0, :], ALU.add, ALU.subtract))
                                            pbuf.use(pslot, em.tok("V"))
                                        v_lo = em.cnt["V"]

                                        sslot = st_i % 4

                                        @block.scalar
                                        def _(a):
                                            em.wait(a, "V", v_lo)
                                            bi = SPa.batch(a)
                                            SPa.inc(bi, a.dma_start(
                                                out=dst[s, jj, :, :, g * TG:(g + 1) * TG]
                                                .rearrange("l p t -> p l t"),
                                                in_=st_g[ss][:]))
                                            stbuf.use(ss, SPa.tok(bi))
                                            sqbuf.wait_reuse(a, sslot)
                                            em.inc("A", a.activation(sq_g[sslot][:], st_g[ss][:, 0, :],
                                                                     AF.Square))
                                            stbuf.use(ss, em.tok("A"))
                                        pending_norm.append((em.cnt["A"], sslot,
                                                             (s * H + jj) * NG + g, normbuf))
                                    st_i += 1
                            else:
                                for tc in range(TPG):
                                    for ot in range(WH * 128 // OT):
                                        pslot = st_i % 2
                                        pb = ps_p[pslot]

                                        @block.tensor
                                        def _(t):
                                            wait_toks(t, [tok_w, tok_a])
                                            pbuf.wait_reuse(t, pslot)
                                            first = True
                                            for c in range(NCH):
                                                ah = act_g[aslot][:, c, 0, tc * 128:(tc + 1) * 128]
                                                al = act_g[aslot][:, c, 1, tc * 128:(tc + 1) * 128]
                                                wh = wt_sb[:, c, 0, ot * OT:(ot + 1) * OT]
                                                wl = wt_sb[:, c, 1, ot * OT:(ot + 1) * OT]
                                                t.matmul(pb[:, 0:OT], ah, wh, start=first, stop=False)
                                                t.matmul(pb[:, 0:OT], al, wh, start=False, stop=False)
                                                mi = t.matmul(pb[:, 0:OT], ah, wl, start=False,
                                                              stop=(c == NCH - 1))
                                                first = False
                                            em.inc("T", mi)
                                            abuf.use(aslot, em.tok("T"))
                                            wbuf.use(wslot, em.tok("T"))
                                        t_mm = em.cnt["T"]
                                        vslot = 0
                                        ss = st_i % 4

                                        @block.vector
                                        def _(v):
                                            em.wait(v, "T", t_mm)
                                            stvbuf.wait_reuse(v, vslot)
                                            stbuf.wait_reuse(v, ss)
                                            v.drain()
                                            obase = half * WH * 128 + ot * OT
                                            v.tensor_tensor(stv_g[vslot][:, 0:OT], pb[:, 0:OT],
                                                            vbb[:, obase:obase + OT], ALU.add)
                                            v.drain()
                                            v.tensor_copy(st_g[ss][:, 0, 0:OT], stv_g[vslot][:, 0:OT])
                                            v.drain()
                                            em.inc("V", v.tensor_tensor(
                                                st_g[ss][:, 1, 0:OT], stv_g[vslot][:, 0:OT],
                                                st_g[ss][:, 0, 0:OT], ALU.subtract))
                                            pbuf.use(pslot, em.tok("V"))
                                            stvbuf.use(vslot, em.tok("V"))
                                        v_ev = em.cnt["V"]

                                        @block.gpsimd
                                        def _(gp):
                                            em.wait(gp, "V", v_ev)
                                            a_blk = (g * TG + tc * 128) // 128
                                            bi = SPg.batch(gp)
                                            for hh in range(OT // HD):
                                                h = (half * WH * 128 + ot * OT) // HD + hh
                                                SPg.inc(bi, gp.dma_start(
                                                    out=sc_V[s, h, :, a_blk, :]
                                                    .rearrange("p (l d) -> p l d", l=2),
                                                    in_=st_g[ss][:, :, hh * HD:(hh + 1) * HD]))
                                            stbuf.use(ss, SPg.tok(bi))
                                        st_i += 1
            emit_pending_norms(drainall=True)

        phase_barrier()

        with contextlib.ExitStack() as pp:
            q_sb = [sb(f"p2_q{i}", [128, 2, TG], BF16, stack=pp) for i in range(2)]
            k_sb = [sb(f"p2_k{i}", [128, 2, S], BF16, stack=pp) for i in range(2)]
            v_sb = [sb(f"p2_v{i}", [128, NKT, 2 * HD], BF16, stack=pp) for i in range(2)]
            atf = [sb(f"p2_atf{i}", [128, KG * TG], F32, stack=pp) for i in range(4)]
            atp = [sb(f"p2_atp{i}", [128, 2, KG * TG], BF16, stack=pp) for i in range(4)]
            avtf = sb("p2_avtf", [128, TG], F32, stack=pp)
            avu = [sb(f"p2_avu{i}", [128, TG], F32, stack=pp) for i in range(2)]
            avt_sb = sb("p2_avt", [128, H, 2, TG], BF16, stack=pp)
            g_sb = [sb(f"p2_g{i}", [128, TG], F32, stack=pp) for i in range(2)]
            xo_sb = sb("p2_xo", [128, TPG, D], stack=pp)
            ow_sb = [sb(f"p2_ow{i}", [128, NCH, 2, 128], BF16, stack=pp) for i in range(2)]
            p_st = [sb(f"p2_pst{i}", [128, TG], F32, stack=pp) for i in range(2)]
            rc_row = sb("p2_rc", [1, TG], F32, stack=pp)
            rc_b = sb("p2_rcb", [128, TG], F32, stack=pp)
            mask01 = sb("mask01_s", [128, MW], stack=pp)
            NSLOT = 2
            ps_sc = pp.enter_context(nc.psum_tensor("p2_sc", [128, NSLOT * KG * TG], F32))
            ps_av2 = [pp.enter_context(nc.psum_tensor(f"p2_av{i}", [128, TG], F32)) for i in range(2)]
            ps_sum2 = [pp.enter_context(nc.psum_tensor(f"p2_sum{i}", [1, TG], F32)) for i in range(2)]
            negc_bb = sb("p2_negbb", [128, NSEQ * H], stack=pp)
            kvbuf = Buf(2)
            qbuf = Buf(2)
            aubuf = Buf(2)
            rcrbuf = Buf(1)
            atfbuf = Buf(4)
            atpbuf = Buf(4)
            owbuf = Buf(2)
            pstbuf = Buf(2)
            gbuf = Buf(2)
            xgbuf = Buf(1)

            for s in range(NSEQ):
                for h in range(H):
                    idx = s * H + h

                    @block.vector
                    def _(v):
                        em.wait(v, "A")
                        v.drain()
                        base = idx * NG
                        v.tensor_reduce(spool[:, 1:2], normq[:, base:base + NG], axis=AX.X, op=ALU.max)
                        v.tensor_reduce(spool[:, 2:3], normk[:, base:base + NG], axis=AX.X, op=ALU.max)
                        v.drain()
                        em.inc("V", v.tensor_tensor(spool[:, 3:4], spool[:, 1:2],
                                                    spool[:, 2:3], ALU.mult))
                    vv = em.cnt["V"]

                    @block.scalar
                    def _(a):
                        em.wait(a, "V", vv)
                        a.drain()
                        em.inc("A", a.activation(spool[:, 4:5], spool[:, 3:4], AF.Sqrt))
                    aa = em.cnt["A"]

                    @block.vector
                    def _(v):
                        em.wait(v, "A", aa)
                        v.drain()
                        em.inc("V", v.tensor_scalar(negC[:, idx:idx + 1], spool[:, 4:5],
                                                    -cfg.SCALE, None, ALU.mult))

            @block.scalar
            def _(a):
                em.wait(a, "V")
                bi = SPa.batch(a)
                SPa.inc(bi, a.dma_start(out=sc_negc[:], in_=negC[:]))
                cell["t"] = SPa.tok(bi)
            tok_ncst = cell["t"]

            @block.sync
            def _(sync):
                wait_toks(sync, [tok_ncst])
                bi = LP.batch(sync)
                LP.inc(bi, sync.dma_start(out=negc_bb[:],
                                          in_=sc_negc.partition_broadcast(128)
                                          .rearrange("p a t -> p (a t)")))
                cell["t"] = LP.tok(bi)
            tok_ncbb = cell["t"]

            @block.sync
            def _(sync):
                bi = LP.batch(sync)
                LP.inc(bi, sync.dma_start(out=mask01[:], in_=mask_h[:]))
                cell["t"] = LP.tok(bi)
            tok_mask = cell["t"]

            @block.vector
            def _(v):
                wait_toks(v, [tok_mask])
                em.inc("V", v.drain())

            epfree = [None]
            for s in range(NSEQ):
                for m in range(NMT):
                    block.flush()
                    nkt = (m + 1) * TG // 128
                    ngrp = (nkt + KG - 1) // KG

                    @block.sync
                    def _(sync):
                        xgbuf.wait_reuse(sync, 0)
                        bi = LP.batch(sync)
                        LP.inc(bi, sync.dma_start(
                            out=xo_sb[:], in_=x_h[s, m * TG:(m + 1) * TG, :]
                            .rearrange("(a p) d -> p a d", p=128)))
                        cell["t"] = LP.tok(bi)
                    tok_xg = cell["t"]

                    def load_kv(h):
                        kvslot = h % 2

                        @block.sync
                        def _(sync):
                            kvbuf.wait_reuse(sync, kvslot)
                            qbuf.wait_reuse(sync, kvslot)
                            bi = LP.batch(sync)
                            LP.inc(bi, sync.dma_start(
                                out=q_sb[kvslot][:], in_=sc_QT[s, h, :, :, m * TG:(m + 1) * TG]
                                .rearrange("l p t -> p l t")))
                            LP.inc(bi, sync.dma_start(
                                out=k_sb[kvslot][:, :, 0:nkt * 128],
                                in_=sc_KT[s, h, :, :, 0:nkt * 128]
                                .rearrange("l p t -> p l t")))
                            LP.inc(bi, sync.dma_start(
                                out=v_sb[kvslot][:, 0:nkt, :],
                                in_=sc_V[s, h, :, 0:nkt, :]))
                            cell["t"] = LP.tok(bi)
                        return cell["t"]

                    def emit_scores(h, kg, gi, tok_kv):
                        kvslot = h % 2
                        j0 = kg * KG
                        j1 = min(nkt, j0 + KG)
                        gslot = gi % 2
                        base = gslot * KG * TG

                        @block.tensor
                        def _(t):
                            wait_toks(t, [tok_kv])
                            if gi < 2 and epfree[0] is not None:
                                wait_toks(t, [epfree[0]])
                            if exp_toks[gslot] is not None:
                                wait_toks(t, [exp_toks[gslot]])
                            if kg == 0:
                                if avfree[h % 2] is not None:
                                    wait_toks(t, [avfree[h % 2], sumfree[h % 2]])
                            for j in range(j0, j1):
                                off = base + (j - j0) * TG
                                kh = k_sb[kvslot][:, 0, j * 128:(j + 1) * 128]
                                kl = k_sb[kvslot][:, 1, j * 128:(j + 1) * 128]
                                t.matmul(ps_sc[:, off:off + TG], kh, q_sb[kvslot][:, 0, :],
                                         start=True, stop=False)
                                t.matmul(ps_sc[:, off:off + TG], kh, q_sb[kvslot][:, 1, :],
                                         start=False, stop=False)
                                em.inc("T", t.matmul(ps_sc[:, off:off + TG], kl,
                                                     q_sb[kvslot][:, 0, :],
                                                     start=False, stop=True))
                        return em.tok("T")

                    def emit_exp(h, kg, gi, t_sc):
                        j0 = kg * KG
                        j1 = min(nkt, j0 + KG)
                        w = j1 - j0
                        gslot = gi % 2
                        aslot = gi % 4
                        base = gslot * KG * TG

                        @block.scalar
                        def _(a):
                            wait_toks(a, [t_sc, tok_ncbb])
                            atfbuf.wait_reuse(a, aslot)
                            a.drain()
                            em.inc("A", a.activation(atf[aslot][:, 0:w * TG],
                                                     ps_sc[:, base:base + w * TG], AF.Exp,
                                                     scale=cfg.SCALE,
                                                     bias=negc_bb[:, s * H + h:s * H + h + 1]))
                        a_exp = em.cnt["A"]
                        exp_toks[gslot] = (em.sems["A"], a_exp)

                        masked = [j for j in range(j0, j1) if j - m * TPG >= 0]
                        g_mask = None
                        if masked:
                            @block.gpsimd
                            def _(gp):
                                em.wait(gp, "A", a_exp)
                                wait_toks(gp, [tok_mask])
                                for j in masked:
                                    jj = j - m * TPG
                                    off = (TG - 128) - 128 * jj
                                    fo = (j - j0) * TG
                                    mi = gp.tensor_tensor(atf[aslot][:, fo:fo + TG],
                                                          atf[aslot][:, fo:fo + TG],
                                                          mask01[:, off:off + TG], ALU.mult)
                                em.inc("G", mi)
                            g_mask = em.cnt["G"]

                        @block.scalar
                        def _(a):
                            if g_mask is not None:
                                em.wait(a, "G", g_mask)
                            atpbuf.wait_reuse(a, aslot)
                            a.drain()
                            em.inc("A", a.activation(atp[aslot][:, 0, 0:w * TG],
                                                     atf[aslot][:, 0:w * TG], AF.Identity))
                        a_hi = em.cnt["A"]

                        @block.vector
                        def _(v):
                            em.wait(v, "A", a_hi)
                            v.drain()
                            em.inc("V", v.tensor_tensor(atp[aslot][:, 1, 0:w * TG],
                                                        atf[aslot][:, 0:w * TG],
                                                        atp[aslot][:, 0, 0:w * TG],
                                                        ALU.subtract))
                            atfbuf.use(aslot, em.tok("V"))
                        return em.tok("V")

                    def emit_av(h, kg, gi, split_tok):
                        kvslot = h % 2
                        j0 = kg * KG
                        j1 = min(nkt, j0 + KG)
                        aslot = gi % 4
                        pav = ps_av2[h % 2]
                        psum_ = ps_sum2[h % 2]

                        @block.tensor
                        def _(t):
                            wait_toks(t, [split_tok])
                            for j in range(j0, j1):
                                fo = (j - j0) * TG
                                ah = atp[aslot][:, 0, fo:fo + TG]
                                al = atp[aslot][:, 1, fo:fo + TG]
                                vh = v_sb[kvslot][:, j, 0:HD]
                                vl = v_sb[kvslot][:, j, HD:2 * HD]
                                first = (j == 0)
                                last = (j == nkt - 1)
                                t.matmul(psum_[:], onesb[:], ah,
                                         start=first, stop=False, skip_group_check=True)
                                em.inc("T", t.matmul(psum_[:], onesb[:], al,
                                                     start=False, stop=last,
                                                     skip_group_check=True))
                                t.matmul(pav[:], vh, ah, start=first, stop=False,
                                         skip_group_check=True)
                                t.matmul(pav[:], vh, al, start=False, stop=False,
                                         skip_group_check=True)
                                em.inc("T", t.matmul(pav[:], vl, ah, start=False, stop=last,
                                                     skip_group_check=True))
                            atpbuf.use(aslot, em.tok("T"))
                            kvbuf.use(kvslot, em.tok("T"))
                            qbuf.use(kvslot, em.tok("T"))
                        return em.tok("T")

                    def head_tail(h, t_av):
                        auslot = h % 2
                        pav = ps_av2[h % 2]
                        psum_ = ps_sum2[h % 2]

                        @block.scalar
                        def _(a):
                            wait_toks(a, [t_av])
                            aubuf.wait_reuse(a, auslot)
                            a.drain()
                            em.inc("A", a.activation(avu[auslot][:], pav[:], AF.Identity))
                        avfree[h % 2] = em.tok("A")

                        @block.vector
                        def _(v):
                            wait_toks(v, [t_av])
                            rcrbuf.wait_reuse(v, 0)
                            v.drain()
                            em.inc("V", v.reciprocal(rc_row[:], psum_[:]))
                        v_rc = em.cnt["V"]
                        sumfree[h % 2] = em.tok("V")

                        @block.gpsimd
                        def _(gp):
                            em.wait(gp, "V", v_rc)
                            bi = SPg.batch(gp)
                            SPg.inc(bi, gp.dma_start(out=sc_rc[s, m, h, :].unsqueeze(0),
                                                     in_=rc_row[:]))
                            cell["t"] = SPg.tok(bi)
                        tok_rcst = cell["t"]
                        rcrbuf.use(0, tok_rcst)

                        @block.gpsimd
                        def _(gp):
                            wait_toks(gp, [tok_rcst])
                            bi = SPg.batch(gp)
                            SPg.inc(bi, gp.dma_start(
                                out=rc_b[:], in_=sc_rc[s, m, h, :]
                                .unsqueeze(0).partition_broadcast(128)
                                .rearrange("p a t -> p (a t)")))
                            cell["t"] = SPg.tok(bi)
                        tok_rcb = cell["t"]

                        @block.gpsimd
                        def _(gp):
                            wait_toks(gp, [tok_rcb, avfree[h % 2]])
                            gp.tensor_tensor(avtf[:], avu[auslot][:], rc_b[:], ALU.mult)
                            gp.drain()
                            em.inc("G", gp.tensor_copy(avt_sb[:, h, 0, :], avtf[:]))
                            aubuf.use(auslot, em.tok("G"))
                            gp.drain()
                            em.inc("G", gp.tensor_tensor(avt_sb[:, h, 1, :], avtf[:],
                                                         avt_sb[:, h, 0, :], ALU.subtract))

                    exp_toks = [None, None]
                    avfree = [None, None]
                    sumfree = [None, None]
                    pend = []
                    gi = 0

                    def pop_one():
                        hh, kk, gg_, sp = pend.pop(0)
                        t_av = emit_av(hh, kk, gg_, sp)
                        if kk == ngrp - 1:
                            head_tail(hh, t_av)

                    for h in range(H):
                        while pend and pend[0][0] <= h - 2:
                            pop_one()
                        tok_kv = load_kv(h)
                        for kg in range(ngrp):
                            t_sc = emit_scores(h, kg, gi, tok_kv)
                            if len(pend) >= 3:
                                pop_one()
                            pend.append((h, kg, gi, emit_exp(h, kg, gi, t_sc)))
                            gi += 1
                    while pend:
                        pop_one()
                    v_avt = em.cnt["V"]
                    g_avt = em.cnt["G"]
                    block.flush()

                    for j in range(NCH):
                        owslot = j % 2
                        gslot = j % 2

                        @block.sync
                        def _(sync):
                            owbuf.wait_reuse(sync, owslot)
                            gbuf.wait_reuse(sync, gslot)
                            bi = LP.batch(sync)
                            LP.inc(bi, sync.dma_start(out=ow_sb[owslot][:], in_=sc_oT[j]))
                            LP.inc(bi, sync.dma_start(
                                out=g_sb[gslot][:],
                                in_=sc_g[s, :, j, m * TG:(m + 1) * TG]))
                            cell["t"] = LP.tok(bi)
                        tok_ow = cell["t"]
                        pob = ps_sc[:, (j % 2) * TG:(j % 2 + 1) * TG]

                        @block.tensor
                        def _(t):
                            wait_toks(t, [tok_ow])
                            em.wait(t, "V", v_avt)
                            em.wait(t, "G", g_avt)
                            em.wait(t, "V")
                            first = True
                            for c in range(NCH):
                                oh = ow_sb[owslot][:, c, 0, :]
                                ol = ow_sb[owslot][:, c, 1, :]
                                ah = avt_sb[:, c, 0, :]
                                al = avt_sb[:, c, 1, :]
                                t.matmul(pob, oh, ah, start=first, stop=False)
                                t.matmul(pob, oh, al, start=False, stop=False)
                                mi = t.matmul(pob, ol, ah, start=False, stop=(c == NCH - 1))
                                first = False
                            em.inc("T", mi)
                            owbuf.use(owslot, em.tok("T"))
                        t_o = em.cnt["T"]
                        pslot = j % 2

                        @block.vector
                        def _(v):
                            em.wait(v, "T", t_o)
                            wait_toks(v, [tok_xg])
                            pstbuf.wait_reuse(v, pslot)
                            v.drain()
                            em.inc("V", v.scalar_tensor_tensor(p_st[pslot][:], pob,
                                                               ob[:, j:j + 1], g_sb[gslot][:],
                                                               ALU.add, ALU.mult))
                            xgbuf.use(0, em.tok("V"))
                            gbuf.use(gslot, em.tok("V"))
                        v_p = em.cnt["V"]

                        @block.tensor
                        def _(t):
                            em.wait(t, "V", v_p)
                            for tt in range(TPG):
                                t.transpose(ps_sc[:, 2 * TG + tt * 128:2 * TG + (tt + 1) * 128],
                                            p_st[pslot][:, tt * 128:(tt + 1) * 128], ident[:])
                            em.inc("T", t.drain())
                            pstbuf.use(pslot, em.tok("T"))
                        t_tp = em.cnt["T"]

                        @block.vector
                        def _(v):
                            em.wait(v, "T", t_tp)
                            v.drain()
                            em.inc("V", v.tensor_tensor(
                                xo_sb[:, :, j * 128:(j + 1) * 128],
                                ps_sc[:, 2 * TG:3 * TG].rearrange("p (a f) -> p a f", a=TPG),
                                xo_sb[:, :, j * 128:(j + 1) * 128],
                                ALU.add))
                            xgbuf.use(0, em.tok("V"))
                    v_out = em.cnt["V"]
                    epfree[0] = em.tok("V")

                    @block.gpsimd
                    def _(gp):
                        em.wait(gp, "V", v_out)
                        bi = OP.batch(gp)
                        OP.inc(bi, gp.dma_start(
                            out=y_h[s, m * TG:(m + 1) * TG, :]
                            .rearrange("(a p) d -> p a d", p=128),
                            in_=xo_sb[:]))
                        xgbuf.use(0, OP.tok(bi))

            @block.sync
            def _(sync):
                OP.wait_all(sync)
            block.flush()

    return nc



_CACHE = {}


def _mask01(TG):
    MW = 2 * TG - 128
    p = np.arange(128)[:, None]
    c = np.arange(MW)[None, :]
    return (c - (TG - 128) >= p).astype(np.float32)


def _chunk_cols(v, NCH):
    return np.ascontiguousarray(np.asarray(v, np.float32).reshape(NCH, 128).T)


def make_in_maps(cfg, x, q_w, q_b, k_w, k_b, v_w, v_b, o_w, o_b, W, ln_w, ln_b,
                 ncores):
    NCH = cfg.NCH
    shared = {
        "qT": np.ascontiguousarray(np.asarray(q_w, np.float32).T),
        "kT": np.ascontiguousarray(np.asarray(k_w, np.float32).T),
        "vT": np.ascontiguousarray(np.asarray(v_w, np.float32).T),
        "oT": np.ascontiguousarray(np.asarray(o_w, np.float32).T),
        "W": np.ascontiguousarray(np.asarray(W, np.float32)),
        "qb": _chunk_cols(q_b, NCH),
        "kb": _chunk_cols(k_b, NCH),
        "ob": _chunk_cols(o_b, NCH),
        "vbrow": np.ascontiguousarray(np.asarray(v_b, np.float32).reshape(1, -1)),
        "lnw": _chunk_cols(ln_w, NCH),
        "lnb": _chunk_cols(ln_b, NCH),
        "id128": np.eye(128, dtype=np.float32),
        "mask01": _mask01(cfg.TG),
    }
    x = np.ascontiguousarray(np.asarray(x, np.float32))
    per = x.shape[0] // ncores
    return [dict(shared, x=np.ascontiguousarray(x[i * per:(i + 1) * per]))
            for i in range(ncores)]


def kernel(L0_attn_x, L0_attn_q_w, L0_attn_q_b, L0_attn_k_w, L0_attn_k_b,
           L0_attn_v_w, L0_attn_v_b, L0_attn_o_w, L0_attn_o_b,
           L0_ffn_W, L0_ffn_ln_w, L0_ffn_ln_b):
    cfg = Cfg()
    if "nc" not in _CACHE:
        _CACHE["nc"] = build(cfg)
    in_maps = make_in_maps(cfg, L0_attn_x, L0_attn_q_w, L0_attn_q_b, L0_attn_k_w,
                           L0_attn_k_b, L0_attn_v_w, L0_attn_v_b, L0_attn_o_w,
                           L0_attn_o_b, L0_ffn_W, L0_ffn_ln_w, L0_ffn_ln_b, NCORES)
    res = run_bass_kernel_spmd(_CACHE["nc"], in_maps, core_ids=list(range(NCORES)))
    return np.concatenate([np.asarray(r["y"], np.float32) for r in res.results], axis=0)


# version 23
# speedup vs baseline: 1.1057x; 1.0681x over previous
import math
import contextlib
import numpy as np

import concourse.bass as bass
import concourse.mybir as mybir
from concourse.bass_utils import run_bass_kernel_spmd

F32 = mybir.dt.float32
BF16 = mybir.dt.bfloat16
AF = mybir.ActivationFunctionType
ALU = mybir.AluOpType
AX = mybir.AxisListType

EPS = 1e-5
NCORES = 8


class Cfg:
    def __init__(self, S=2048, D=2048, H=16, NSEQ=2, WH=None):
        self.S, self.D, self.H, self.NSEQ = S, D, H, NSEQ
        self.HD = D // H
        assert self.HD == 128
        self.SCALE = 1.0 / math.sqrt(self.HD)
        self.NCH = D // 128
        self.TG = 512
        assert S % self.TG == 0
        self.NG = S // self.TG
        self.TPG = self.TG // 128
        self.WH = min(8, self.NCH) if WH is None else WH
        self.NKT = S // 128


class Emit:

    def __init__(self, nc, es):
        self.sems = {}
        self.cnt = {}
        for n in ["V", "A", "T", "G"]:
            self.sems[n] = es.enter_context(nc.semaphore("sem_" + n))
            self.cnt[n] = 0

    def inc(self, name, instr, amt=1):
        instr.then_inc(self.sems[name], amt)
        self.cnt[name] += amt
        return instr

    def tok(self, name):
        return (self.sems[name], self.cnt[name])

    def wait(self, eng, name, val=None):
        if val is None:
            val = self.cnt[name]
        if val > 0:
            eng.wait_ge(self.sems[name], val)


def wait_toks(eng, toks):
    best = {}
    for sem, val in toks:
        key = id(sem)
        if key not in best or best[key][1] < val:
            best[key] = (sem, val)
    for sem, val in best.values():
        eng.wait_ge(sem, val)


class Pool:

    def __init__(self, nc, es, prefix, n):
        self.sems = [es.enter_context(nc.semaphore(f"{prefix}{i}")) for i in range(n)]
        self.cnt = [0] * n
        self.i = 0

    def batch(self, eng):
        idx = self.i % len(self.sems)
        self.i += 1
        if self.cnt[idx] > 0:
            eng.wait_ge(self.sems[idx], self.cnt[idx])
        return idx

    def inc(self, idx, instr):
        instr.then_inc(self.sems[idx], 16)
        self.cnt[idx] += 16
        return instr

    def tok(self, idx):
        return (self.sems[idx], self.cnt[idx])

    def wait_all(self, eng):
        for j, s in enumerate(self.sems):
            if self.cnt[j] > 0:
                eng.wait_ge(s, self.cnt[j])


class Buf:

    def __init__(self, nslots):
        self.n = nslots
        self.marks = [[] for _ in range(nslots)]

    def use(self, slot, tok):
        self.marks[slot % self.n].append(tok)

    def wait_reuse(self, eng, slot):
        wait_toks(eng, self.marks[slot % self.n])
        self.marks[slot % self.n] = []


class _RecInstr:
    __slots__ = ("incs",)

    def __init__(self):
        self.incs = []

    def then_inc(self, sem, amt):
        self.incs.append((sem, amt))
        return self


class _Rec:

    def __init__(self):
        self.calls = []

    def __getattr__(self, name):
        def call(*a, **k):
            ri = _RecInstr()
            self.calls.append((name, a, k, ri))
            return ri
        return call

    def replay(self, eng):
        for name, a, k, ri in self.calls:
            instr = getattr(eng, name)(*a, **k)
            for sem, amt in ri.incs:
                instr.then_inc(sem, amt)


class QBlock:

    NAMES = ("sync", "vector", "scalar", "tensor", "gpsimd")

    def __init__(self, block):
        self.block = block
        self.q = {n: [] for n in self.NAMES}

    def _deco(self, n):
        def deco(fn):
            rec = _Rec()
            fn(rec)
            self.q[n].append(rec)
        return deco

    @property
    def sync(self):
        return self._deco("sync")

    @property
    def vector(self):
        return self._deco("vector")

    @property
    def scalar(self):
        return self._deco("scalar")

    @property
    def tensor(self):
        return self._deco("tensor")

    @property
    def gpsimd(self):
        return self._deco("gpsimd")

    def flush(self):
        for n in self.NAMES:
            recs = self.q[n]
            if recs:
                def runner(eng, recs=list(recs)):
                    for rec in recs:
                        rec.replay(eng)
                getattr(self.block, n)(runner)
                self.q[n] = []


def build(cfg: Cfg):
    S, D, H, NSEQ = cfg.S, cfg.D, cfg.H, cfg.NSEQ
    NCH, TG, NG, TPG, WH, NKT = cfg.NCH, cfg.TG, cfg.NG, cfg.TPG, cfg.WH, cfg.NKT
    HD = cfg.HD
    MW = 2 * TG - 128
    NMT = S // TG
    NHALF = NCH // WH
    KG = 2

    nc = bass.Bass("TRN2", target_bir_lowering=False)

    x_h = nc.dram_tensor("x", [NSEQ, S, D], F32, kind="ExternalInput").ap()
    qT_h = nc.dram_tensor("qT", [D, D], F32, kind="ExternalInput").ap()
    kT_h = nc.dram_tensor("kT", [D, D], F32, kind="ExternalInput").ap()
    vT_h = nc.dram_tensor("vT", [D, D], F32, kind="ExternalInput").ap()
    oT_h = nc.dram_tensor("oT", [D, D], F32, kind="ExternalInput").ap()
    W_h = nc.dram_tensor("W", [D, D], F32, kind="ExternalInput").ap()
    qb_h = nc.dram_tensor("qb", [128, NCH], F32, kind="ExternalInput").ap()
    kb_h = nc.dram_tensor("kb", [128, NCH], F32, kind="ExternalInput").ap()
    ob_h = nc.dram_tensor("ob", [128, NCH], F32, kind="ExternalInput").ap()
    vbrow_h = nc.dram_tensor("vbrow", [1, D], F32, kind="ExternalInput").ap()
    lnw_h = nc.dram_tensor("lnw", [128, NCH], F32, kind="ExternalInput").ap()
    lnb_h = nc.dram_tensor("lnb", [128, NCH], F32, kind="ExternalInput").ap()
    id_h = nc.dram_tensor("id128", [128, 128], F32, kind="ExternalInput").ap()
    mask_h = nc.dram_tensor("mask01", [128, MW], F32, kind="ExternalInput").ap()
    y_h = nc.dram_tensor("y", [NSEQ, S, D], F32, kind="ExternalOutput").ap()

    sc_xnT = nc.dram_tensor("sc_xnT", [NSEQ, NG, 128, NCH, 2, TG], BF16).ap()
    sc_uT = nc.dram_tensor("sc_uT", [NSEQ, NG, 128, NCH, 2, TG], BF16).ap()
    sc_QT = nc.dram_tensor("sc_QT", [NSEQ, H, 2, HD, S], BF16).ap()
    sc_KT = nc.dram_tensor("sc_KT", [NSEQ, H, 2, HD, S], BF16).ap()
    sc_V = nc.dram_tensor("sc_V", [NSEQ, H, 128, NKT, 2 * HD], BF16).ap()
    sc_g = nc.dram_tensor("sc_g", [NSEQ, 128, NCH, S], F32).ap()
    sc_Wq = nc.dram_tensor("sc_Wq", [NCH, 128, D], BF16).ap()
    sc_w = {w: nc.dram_tensor(f"sc_w{w}", [NHALF, 128, NCH, 2, WH * 128], BF16).ap()
            for w in ["q", "k", "v"]}
    sc_oT = nc.dram_tensor("sc_oT", [NCH, 128, NCH, 2, 128], BF16).ap()
    sc_negc = nc.dram_tensor("sc_negc", [1, NSEQ * H], F32).ap()
    sc_rc = nc.dram_tensor("sc_rc", [NSEQ, S // 512, H, 512], F32).ap()

    es = contextlib.ExitStack()
    with es:
        em = Emit(nc, es)
        LP = Pool(nc, es, "lp", 20)
        SPa = Pool(nc, es, "spa", 12)
        SPg = Pool(nc, es, "spg", 8)
        OP = Pool(nc, es, "op", 4)
        block = QBlock(es.enter_context(nc.Block()))

        def sb(name, shape, dt=F32, stack=es):
            return stack.enter_context(nc.sbuf_tensor(name, shape, dt))

        ident = sb("ident_s", [128, 128])
        qb = sb("qb_s", [128, NCH])
        kb = sb("kb_s", [128, NCH])
        ob = sb("ob_s", [128, NCH])
        lnw = sb("lnw_s", [128, NCH])
        lnb = sb("lnb_s", [128, NCH])
        lnb_p = sb("lnbp_s", [128, 2, NCH], BF16)
        biasf = sb("biasf_s", [128, NCH])
        vbb = sb("vbb_s", [128, D])
        vbrow = sb("vbrow_s", [1, D])
        onesf = sb("onesf_s", [128, 1])
        onesb = sb("onesb_s", [128, 1], BF16)
        ones_row = sb("onesrow_s", [1, 128])
        normq = sb("normq_s", [1, NSEQ * H * NG])
        normk = sb("normk_s", [1, NSEQ * H * NG])
        vpool_ = sb("vpool_s", [128, 96])
        spool = sb("spool_s", [1, 8])
        negC = sb("negC_s", [1, NSEQ * H])

        cell = {}

        @block.sync
        def _(sync):
            bi = LP.batch(sync)
            for dst, srch in [(ident, id_h), (qb, qb_h),
                              (kb, kb_h), (ob, ob_h), (lnw, lnw_h), (lnb, lnb_h),
                              (vbrow, vbrow_h)]:
                LP.inc(bi, sync.dma_start(out=dst[:], in_=srch[:]))
            cell["t"] = LP.tok(bi)
        tok_misc = cell["t"]

        @block.vector
        def _(v):
            v.memset(onesf[:], 1.0)
            v.memset(ones_row[:], 1.0)
            v.memset(onesb[:], 1.0)
            wait_toks(v, [tok_misc])
            v.drain()
            v.tensor_copy(lnb_p[:, 0, :], lnb[:])
            v.drain()
            em.inc("V", v.tensor_tensor(lnb_p[:, 1, :], lnb[:], lnb_p[:, 0, :], ALU.subtract))
        V_init = em.cnt["V"]

        def phase_barrier():
            block.flush()

            @block.vector
            def _(v):
                em.inc("V", v.drain())

            @block.scalar
            def _(a):
                em.inc("A", a.drain())

            @block.tensor
            def _(t):
                em.inc("T", t.drain())

            @block.gpsimd
            def _(gp):
                em.inc("G", gp.drain())
            marks = {n: em.cnt[n] for n in ["V", "A", "T", "G"]}

            def waits(e, skip=None):
                for n, vv in marks.items():
                    if n != skip:
                        em.wait(e, n, vv)
                LP.wait_all(e)
                SPa.wait_all(e)
                SPg.wait_all(e)
                OP.wait_all(e)

            @block.vector
            def _(v):
                waits(v, "V")

            @block.scalar
            def _(a):
                waits(a, "A")

            @block.tensor
            def _(t):
                waits(t, "T")

            @block.gpsimd
            def _(gp):
                waits(gp, "G")

            @block.sync
            def _(sync):
                waits(sync)
            block.flush()

        wq_store_toks = []
        absacc = sb("w0_acc", [128, 2])
        with contextlib.ExitStack() as pp:
            x_g = [sb(f"p0_x{i}", [128, TPG // 2, D], stack=pp) for i in range(2)]
            xn_g = [sb(f"p0_xn{i}", [128, TPG // 2, D], stack=pp) for i in range(2)]
            v_g = [sb(f"p0_v{i}", [128, TPG // 2, D], stack=pp) for i in range(2)]
            ev = [sb(f"p0_ev{i}", [128, 2, TG // 2], BF16, stack=pp) for i in range(4)]
            ps_t = [pp.enter_context(nc.psum_tensor(f"p0_pt{i}", [128, TG], F32)) for i in range(2)]
            xbuf = Buf(2)
            evbuf = Buf(4)
            bankbuf = Buf(2)
            w_tile = [sb(f"w0_t{i}", [128, D], stack=pp) for i in range(2)]
            w_hi = [sb(f"w0_hi{i}", [128, D], BF16, stack=pp) for i in range(2)]
            w_lo = [sb(f"w0_lo{i}", [128, D], BF16, stack=pp) for i in range(2)]
            ps_b512 = pp.enter_context(nc.psum_tensor("ps_b512", [128, 512], F32))
            wbuf = Buf(2)
            hlbuf = Buf(2)

            for i in range((D + 511) // 512):
                n = min(512, D - i * 512)

                @block.tensor
                def _(t):
                    wait_toks(t, [tok_misc])
                    em.wait(t, "V")
                    em.inc("T", t.matmul(ps_b512[:, 0:n], ones_row[:],
                                         vbrow[:, i * 512:i * 512 + n],
                                         start=True, stop=True))
                t_bc = em.cnt["T"]

                @block.vector
                def _(v):
                    em.wait(v, "T", t_bc)
                    v.drain()
                    em.inc("V", v.tensor_copy(vbb[:, i * 512:i * 512 + n], ps_b512[:, 0:n]))

            _wjob_i = [0]

            def emit_wjob(jid):
                if jid >= 5 * NCH:
                    return
                ci = _wjob_i[0]
                _wjob_i[0] += 1
                if jid < 4 * NCH:
                    wname, wsrc = [("q", qT_h), ("k", kT_h), ("v", vT_h),
                                   ("o", oT_h)][jid // NCH]
                    c = jid % NCH
                else:
                    wname, wsrc = "abs", W_h
                    c = jid - 4 * NCH

                @block.sync
                def _(sync):
                    wbuf.wait_reuse(sync, ci)
                    bi = LP.batch(sync)
                    LP.inc(bi, sync.dma_start(out=w_tile[ci % 2][:],
                                              in_=wsrc[c * 128:(c + 1) * 128, :]))
                    cell["t"] = LP.tok(bi)
                tok_w = cell["t"]

                if wname == "abs":
                    @block.vector
                    def _(v):
                        wait_toks(v, [tok_w])
                        v.drain()
                        v.tensor_reduce(absacc[:, 1:2], w_tile[ci % 2][:], axis=AX.X,
                                        op=ALU.add, apply_absolute_value=True)
                        v.drain()
                        if c == 0:
                            em.inc("V", v.tensor_copy(absacc[:, 0:1], absacc[:, 1:2]))
                        else:
                            em.inc("V", v.tensor_tensor(absacc[:, 0:1], absacc[:, 0:1],
                                                        absacc[:, 1:2], ALU.add))
                        wbuf.use(ci, em.tok("V"))
                    return

                @block.vector
                def _(v):
                    wait_toks(v, [tok_w])
                    hlbuf.wait_reuse(v, ci)
                    v.drain()
                    v.tensor_copy(w_hi[ci % 2][:], w_tile[ci % 2][:])
                    v.drain()
                    em.inc("V", v.tensor_tensor(w_lo[ci % 2][:], w_tile[ci % 2][:],
                                                w_hi[ci % 2][:], ALU.subtract))
                    wbuf.use(ci, em.tok("V"))
                vc = em.cnt["V"]

                @block.gpsimd
                def _(gp):
                    em.wait(gp, "V", vc)
                    bi = SPg.batch(gp)
                    if wname != "o":
                        for half in range(NHALF):
                            lo_ = half * WH * 128
                            hi_ = (half + 1) * WH * 128
                            SPg.inc(bi, gp.dma_start(
                                out=sc_w[wname][half, :, c, 0, :], in_=w_hi[ci % 2][:, lo_:hi_]))
                            SPg.inc(bi, gp.dma_start(
                                out=sc_w[wname][half, :, c, 1, :], in_=w_lo[ci % 2][:, lo_:hi_]))
                    else:
                        for j in range(NCH):
                            SPg.inc(bi, gp.dma_start(
                                out=sc_oT[j, :, c, 0, :], in_=w_hi[ci % 2][:, j * 128:(j + 1) * 128]))
                            SPg.inc(bi, gp.dma_start(
                                out=sc_oT[j, :, c, 1, :], in_=w_lo[ci % 2][:, j * 128:(j + 1) * 128]))
                    hlbuf.use(ci, SPg.tok(bi))
                    wq_store_toks.append(SPg.tok(bi))
            TPS = TPG // 2
            xnvbuf = Buf(2)
            prev_V2 = [0, 0]
            NIT = NSEQ * NG * 2
            jpg = -(-5 * NCH // NIT)
            sg_list = [(s, g, half) for s in range(NSEQ) for g in range(NG) for half in range(2)]

            def stageA(it):
                s, g, half = sg_list[it]
                slot = it % 2
                vp = vpool_[:, (it % 2) * 48:(it % 2) * 48 + 48]
                toff = half * TPS * 128

                @block.sync
                def _(sync):
                    xbuf.wait_reuse(sync, slot)
                    bi = LP.batch(sync)
                    LP.inc(bi, sync.dma_start(
                        out=x_g[slot][:],
                        in_=x_h[s, g * TG + toff:g * TG + toff + TPS * 128, :]
                        .rearrange("(a p) d -> p a d", p=128)))
                    cell["t"] = LP.tok(bi)
                tok_x = cell["t"]

                @block.scalar
                def _(a):
                    wait_toks(a, [tok_x])
                    em.wait(a, "V", prev_V2[slot])
                    xnvbuf.wait_reuse(a, slot)
                    a.drain()
                    for tt in range(TPS):
                        a.activation(xn_g[slot][:, tt, :], x_g[slot][:, tt, :], AF.Square,
                                     accum_out=vp[:, tt:tt + 1])
                    em.inc("A", a.drain())
                a_sq1 = em.cnt["A"]

                @block.vector
                def _(v):
                    wait_toks(v, [tok_x])
                    v.drain()
                    for tt in range(TPS):
                        v.tensor_reduce(vp[:, 4 + tt:5 + tt], x_g[slot][:, tt, :],
                                        axis=AX.X, op=ALU.add)
                    v.drain()
                    v.tensor_scalar(vp[:, 8:8 + TPS], vp[:, 4:4 + TPS], 1.0 / D, None, ALU.mult)
                    em.wait(v, "A", a_sq1)
                    v.tensor_scalar(vp[:, 12:12 + TPS], vp[:, 0:TPS], 1.0 / D, None, ALU.mult)
                    v.drain()
                    v.tensor_tensor(vp[:, 16:16 + TPS], vp[:, 8:8 + TPS],
                                    vp[:, 8:8 + TPS], ALU.mult)
                    v.drain()
                    v.tensor_tensor(vp[:, 12:12 + TPS], vp[:, 12:12 + TPS],
                                    vp[:, 16:16 + TPS], ALU.subtract)
                    v.drain()
                    v.tensor_scalar(vp[:, 12:12 + TPS], vp[:, 12:12 + TPS], EPS, None, ALU.add)
                    v.drain()
                    em.inc("V", v.reciprocal(vp[:, 16:16 + TPS], vp[:, 12:12 + TPS]))
                v_rc1 = em.cnt["V"]

                @block.scalar
                def _(a):
                    em.wait(a, "V", v_rc1)
                    a.drain()
                    em.inc("A", a.activation(vp[:, 16:16 + TPS], vp[:, 16:16 + TPS], AF.Sqrt))
                a_r1 = em.cnt["A"]

                @block.vector
                def _(v):
                    em.wait(v, "A", a_r1)
                    v.drain()
                    em.inc("V", v.scalar_tensor_tensor(vp[:, 20:20 + TPS], vp[:, 8:8 + TPS],
                                                       -1.0, vp[:, 16:16 + TPS],
                                                       ALU.mult, ALU.mult))
                return em.cnt["V"]

            def stageB(it, v_b1):
                s, g, half = sg_list[it]
                slot = it % 2
                vp = vpool_[:, (it % 2) * 48:(it % 2) * 48 + 48]
                toff = half * TPS * 128

                @block.scalar
                def _(a):
                    em.wait(a, "V", v_b1)
                    a.drain()
                    for tt in range(TPS):
                        a.activation(xn_g[slot][:, tt, :], x_g[slot][:, tt, :], AF.Identity,
                                     scale=vp[:, 16 + tt:17 + tt], bias=vp[:, 20 + tt:21 + tt])
                    a.drain()
                    for tt in range(TPS):
                        a.activation(v_g[slot][:, tt, :], xn_g[slot][:, tt, :], AF.Square,
                                     accum_out=vp[:, 24 + tt:25 + tt])
                    em.inc("A", a.drain())
                    xbuf.use(slot, em.tok("A"))
                a_sq2 = em.cnt["A"]

                @block.vector
                def _(v):
                    em.wait(v, "A", a_sq2)
                    v.drain()
                    for tt in range(TPS):
                        v.tensor_reduce(vp[:, 28 + tt:29 + tt], xn_g[slot][:, tt, :],
                                        axis=AX.X, op=ALU.add)
                    v.drain()
                    v.tensor_scalar(vp[:, 32:32 + TPS], vp[:, 28:28 + TPS], 1.0 / D, None, ALU.mult)
                    v.tensor_scalar(vp[:, 36:36 + TPS], vp[:, 24:24 + TPS], 1.0 / D, None, ALU.mult)
                    v.drain()
                    v.tensor_tensor(vp[:, 40:40 + TPS], vp[:, 32:32 + TPS],
                                    vp[:, 32:32 + TPS], ALU.mult)
                    v.drain()
                    v.tensor_tensor(vp[:, 36:36 + TPS], vp[:, 36:36 + TPS],
                                    vp[:, 40:40 + TPS], ALU.subtract)
                    v.drain()
                    v.tensor_scalar(vp[:, 36:36 + TPS], vp[:, 36:36 + TPS], EPS, None, ALU.add)
                    v.drain()
                    em.inc("V", v.reciprocal(vp[:, 40:40 + TPS], vp[:, 36:36 + TPS]))
                v_rc2 = em.cnt["V"]

                @block.scalar
                def _(a):
                    em.wait(a, "V", v_rc2)
                    a.drain()
                    em.inc("A", a.activation(vp[:, 40:40 + TPS], vp[:, 40:40 + TPS], AF.Sqrt))
                a_r2 = em.cnt["A"]

                @block.vector
                def _(v):
                    em.wait(v, "A", a_r2)
                    v.drain()
                    em.inc("V", v.scalar_tensor_tensor(vp[:, 44:44 + TPS], vp[:, 32:32 + TPS],
                                                       -1.0, vp[:, 40:40 + TPS],
                                                       ALU.mult, ALU.mult))
                v_b2 = em.cnt["V"]
                prev_V2[slot] = v_b2

                @block.scalar
                def _(a):
                    em.wait(a, "V", v_b2)
                    a.drain()
                    for tt in range(TPS):
                        a.activation(v_g[slot][:, tt, :], xn_g[slot][:, tt, :], AF.Identity,
                                     scale=vp[:, 40 + tt:41 + tt], bias=vp[:, 44 + tt:45 + tt])
                    em.inc("A", a.drain())
                a_tiles = em.cnt["A"]

                for which, (src_g, dst_sc) in enumerate(
                        [(xn_g, sc_xnT), (v_g, sc_uT)]):
                    for c in range(NCH):
                        bidx = which * NCH + c
                        bank = ps_t[bidx % 2]

                        @block.tensor
                        def _(t):
                            em.wait(t, "A", a_tiles)
                            bankbuf.wait_reuse(t, bidx)
                            for tt in range(TPS):
                                t.transpose(bank[:, tt * 128:(tt + 1) * 128],
                                            src_g[slot][:, tt, c * 128:(c + 1) * 128],
                                            ident[:])
                            em.inc("T", t.drain())
                            xnvbuf.use(slot, em.tok("T"))
                        t_tr = em.cnt["T"]
                        eslot = bidx % 4

                        @block.scalar
                        def _(a):
                            em.wait(a, "T", t_tr)
                            evbuf.wait_reuse(a, eslot)
                            a.drain()
                            if which == 0:
                                a.activation(ev[eslot][:, 0, :], bank[:, 0:TPS * 128],
                                             AF.Identity)
                            else:
                                a.activation(ev[eslot][:, 0, :], bank[:, 0:TPS * 128],
                                             AF.Identity, scale=lnw[:, c:c + 1])
                            em.inc("A", a.drain())
                        a_hi = em.cnt["A"]

                        @block.vector
                        def _(v):
                            em.wait(v, "A", a_hi)
                            v.drain()
                            if which == 0:
                                em.inc("V", v.tensor_tensor(ev[eslot][:, 1, :],
                                                            bank[:, 0:TPS * 128],
                                                            ev[eslot][:, 0, :], ALU.subtract))
                            else:
                                em.inc("V", v.scalar_tensor_tensor(
                                    ev[eslot][:, 1, :], bank[:, 0:TPS * 128],
                                    lnw[:, c:c + 1],
                                    ev[eslot][:, 0, :], ALU.mult, ALU.subtract))
                            bankbuf.use(bidx, em.tok("V"))
                        v_lo = em.cnt["V"]

                        @block.scalar
                        def _(a):
                            em.wait(a, "V", v_lo)
                            bi = SPa.batch(a)
                            SPa.inc(bi, a.dma_start(
                                out=dst_sc[s, g, :, c, :, toff:toff + TPS * 128],
                                in_=ev[eslot][:]))
                            evbuf.use(eslot, SPa.tok(bi))
                for jid in range(it * jpg, (it + 1) * jpg):
                    emit_wjob(jid)
                block.flush()

            vb1 = stageA(0)
            for it in range(NIT):
                nxt = stageA(it + 1) if it + 1 < NIT else None
                stageB(it, vb1)
                vb1 = nxt
        phase_barrier()

        with contextlib.ExitStack() as pp:
            w_tile = [sb(f"wb_t{i}", [128, D], stack=pp) for i in range(2)]
            wq_sb = sb("wb_wq", [128, NCH, D], BF16, stack=pp)
            thr_b = sb("wb_thr", [128, 1], stack=pp)
            mneg = sb("wb_mneg", [128, D], stack=pp)
            ps_w0 = pp.enter_context(nc.psum_tensor("ps_w0", [1, 1], F32))
            ps_thr = pp.enter_context(nc.psum_tensor("ps_thr", [128, 1], F32))
            ps_bias = pp.enter_context(nc.psum_tensor("ps_bias", [128, NCH], F32))
            wbuf = Buf(2)

            @block.tensor
            def _(t):
                em.wait(t, "V")
                em.inc("T", t.matmul(ps_w0[:], absacc[:, 0:1], onesf[:], start=True, stop=True))
            T_sum = em.cnt["T"]

            @block.vector
            def _(v):
                em.wait(v, "T", T_sum)
                v.drain()
                em.inc("V", v.tensor_scalar(spool[:, 0:1], ps_w0[:], 0.5 / (D * D), None, ALU.mult))
            V_thr = em.cnt["V"]

            @block.tensor
            def _(t):
                em.wait(t, "V", V_thr)
                em.inc("T", t.matmul(ps_thr[:], ones_row[:], spool[:, 0:1],
                                     start=True, stop=True))
            T_thr = em.cnt["T"]

            @block.vector
            def _(v):
                em.wait(v, "T", T_thr)
                v.drain()
                em.inc("V", v.tensor_copy(thr_b[:], ps_thr[:]))
            V_thrb = em.cnt["V"]

            for c in range(NCH):
                @block.sync
                def _(sync):
                    wbuf.wait_reuse(sync, c)
                    bi = LP.batch(sync)
                    LP.inc(bi, sync.dma_start(out=w_tile[c % 2][:],
                                              in_=W_h[c * 128:(c + 1) * 128, :]))
                    cell["t"] = LP.tok(bi)
                tok_w = cell["t"]

                @block.vector
                def _(v):
                    wait_toks(v, [tok_w])
                    em.wait(v, "V", V_thrb)
                    v.drain()
                    v.tensor_scalar(mneg[:], w_tile[c % 2][:], thr_b[:], 0.0, ALU.add, ALU.is_lt)
                    v.drain()
                    em.inc("V", v.scalar_tensor_tensor(wq_sb[:, c, :], w_tile[c % 2][:], thr_b[:],
                                                       mneg[:], ALU.is_gt, ALU.subtract))
                    wbuf.use(c, em.tok("V"))
                vc = em.cnt["V"]

                @block.gpsimd
                def _(gp):
                    em.wait(gp, "V", vc)
                    bi = SPg.batch(gp)
                    SPg.inc(bi, gp.dma_start(out=sc_Wq[c], in_=wq_sb[:, c, :]))
                    wq_store_toks.append(SPg.tok(bi))
            V_wq = em.cnt["V"]

            @block.tensor
            def _(t):
                em.wait(t, "V", max(V_wq, V_init))
                for j in range(NCH):
                    for c in range(NCH):
                        for hl in range(2):
                            t.matmul(ps_bias[:, j:j + 1], wq_sb[:, c, j * 128:(j + 1) * 128],
                                     lnb_p[:, hl, c:c + 1],
                                     start=(c == 0 and hl == 0),
                                     stop=(c == NCH - 1 and hl == 1),
                                     skip_group_check=True)
                em.inc("T", t.drain())
            T_bias = em.cnt["T"]

            @block.vector
            def _(v):
                em.wait(v, "T", T_bias)
                v.drain()
                em.inc("V", v.tensor_copy(biasf[:], ps_bias[:]))

            @block.gpsimd
            def _(gp):
                wait_toks(gp, wq_store_toks)
                em.wait(gp, "T", T_bias)
                em.inc("G", gp.drain())

        phase_barrier()

        with contextlib.ExitStack() as pp:
            wt_sb_ = [sb(f"p1_w{i}", [128, NCH, 2, WH * 128], BF16, stack=pp) for i in range(1)]
            act_g = [sb(f"p1_a{i}", [128, NCH, 2, TG], BF16, stack=pp) for i in range(2)]
            st_g = [sb(f"p1_s{i}", [128, 2, TG], BF16, stack=pp) for i in range(4)]
            stv_g = [sb("p1_sv0", [128, TG], F32, stack=pp)]
            stf_g = [sb("p1_sf0", [128, TG], F32, stack=pp)]
            sq_g = [sb(f"p1_sq{i}", [128, TG], BF16, stack=pp) for i in range(4)]
            ps_p = [pp.enter_context(nc.psum_tensor(f"p1_p{i}", [128, TG], F32)) for i in range(2)]
            ps_n = [pp.enter_context(nc.psum_tensor(f"p1_n{i}", [1, TG], F32)) for i in range(2)]
            ps_n_buf = Buf(2)
            abuf = Buf(2)
            stbuf = Buf(4)
            stvbuf = Buf(1)
            stfbuf = Buf(1)
            wbuf = Buf(1)
            pbuf = Buf(2)

            OT = min(512, WH * 128)
            sqbuf = Buf(4)
            pending_norm = []
            nrm_i = [0]

            def emit_pending_norms(drainall=False):
                while (len(pending_norm) > 2) or (drainall and pending_norm):
                    a_ev, sslot, pos, normbuf = pending_norm.pop(0)
                    ni = nrm_i[0]
                    nrm_i[0] += 1

                    @block.tensor
                    def _(t):
                        em.wait(t, "A", a_ev)
                        ps_n_buf.wait_reuse(t, ni)
                        em.inc("T", t.matmul(ps_n[ni % 2][:], onesb[:], sq_g[sslot][:],
                                             start=True, stop=True))
                        sqbuf.use(sslot, em.tok("T"))
                    t_n = em.cnt["T"]

                    @block.vector
                    def _(v):
                        em.wait(v, "T", t_n)
                        v.drain()
                        em.inc("V", v.tensor_reduce(normbuf[:, pos:pos + 1], ps_n[ni % 2][:],
                                                    axis=AX.X, op=ALU.max))
                        ps_n_buf.use(ni, em.tok("V"))

            jobs = [("q", "q", qb, sc_QT, normq, sc_xnT),
                    ("k", "k", kb, sc_KT, normk, sc_xnT),
                    ("v", "v", None, sc_V, None, sc_xnT),
                    ("f", None, None, sc_g, None, sc_uT)]
            st_i = 0
            w_i = 0
            for wname, wkey, bias_cols, dst, normbuf, actsrc in jobs:
                nhalf_w = 1 if (wname == "f" and NHALF == 2) else NHALF
                for half in range(nhalf_w):
                    wslot = 0
                    wt_sb = wt_sb_[wslot]
                    w_i += 1

                    @block.sync
                    def _(sync):
                        wbuf.wait_reuse(sync, wslot)
                        bi = LP.batch(sync)
                        if wkey is not None:
                            LP.inc(bi, sync.dma_start(out=wt_sb[:], in_=sc_w[wkey][half]))
                        elif NHALF == 2:
                            LP.inc(bi, sync.dma_start(
                                out=wt_sb[:].rearrange("p c l o -> p c (l o)"),
                                in_=sc_Wq.rearrange("c p o -> p c o")))
                        else:
                            LP.inc(bi, sync.dma_start(
                                out=wt_sb[:, :, 0, :],
                                in_=sc_Wq[:, :, half * WH * 128:(half + 1) * WH * 128]
                                .rearrange("c p o -> p c o")))
                        cell["t"] = LP.tok(bi)
                    tok_w = cell["t"]

                    for s in range(NSEQ):
                        for g in range(NG):
                            if (s * NG + g) % 2 == 0:
                                block.flush()
                            aslot = st_i % 2

                            @block.sync
                            def _(sync):
                                abuf.wait_reuse(sync, aslot)
                                bi = LP.batch(sync)
                                LP.inc(bi, sync.dma_start(
                                    out=act_g[aslot][:], in_=actsrc[s, g]))
                                cell["t"] = LP.tok(bi)
                            tok_a = cell["t"]

                            if wname != "v":
                                jrange = NCH if (wname == "f" and NHALF == 2) else WH
                                for j in range(jrange):
                                    jj = half * WH + j if wname != "f" else j
                                    pslot = st_i % 2
                                    pb = ps_p[pslot]

                                    emit_pending_norms()

                                    @block.tensor
                                    def _(t):
                                        wait_toks(t, [tok_w, tok_a])
                                        pbuf.wait_reuse(t, pslot)
                                        first = True
                                        for c in range(NCH):
                                            if wname == "f" and NHALF == 2:
                                                wq_f = wt_sb[:, c, j // WH,
                                                             (j % WH) * 128:(j % WH + 1) * 128]
                                            elif wname == "f":
                                                wq_f = wt_sb[:, c, 0, j * 128:(j + 1) * 128]
                                            wh = wt_sb[:, c, 0, j * 128:(j + 1) * 128] if wname != "f" else None
                                            wl = wt_sb[:, c, 1, j * 128:(j + 1) * 128] if wname != "f" else None
                                            ah = act_g[aslot][:, c, 0, :]
                                            al = act_g[aslot][:, c, 1, :]
                                            if wname == "f":
                                                t.matmul(pb[:], wq_f, ah, start=first, stop=False)
                                                mi = t.matmul(pb[:], wq_f, al, start=False,
                                                              stop=(c == NCH - 1))
                                            else:
                                                t.matmul(pb[:], wh, ah, start=first, stop=False)
                                                t.matmul(pb[:], wh, al, start=False, stop=False)
                                                mi = t.matmul(pb[:], wl, ah, start=False,
                                                              stop=(c == NCH - 1))
                                            first = False
                                        em.inc("T", mi)
                                        abuf.use(aslot, em.tok("T"))
                                        wbuf.use(wslot, em.tok("T"))
                                    t_mm = em.cnt["T"]
                                    ss = st_i % 4

                                    if wname == "f":
                                        fslot = 0

                                        @block.scalar
                                        def _(a):
                                            em.wait(a, "T", t_mm)
                                            stfbuf.wait_reuse(a, fslot)
                                            a.drain()
                                            a.activation(stf_g[fslot][:], pb[:], AF.Silu,
                                                         bias=biasf[:, jj:jj + 1])
                                            em.inc("A", a.drain())
                                            pbuf.use(pslot, em.tok("A"))
                                            bi = SPa.batch(a)
                                            SPa.inc(bi, a.dma_start(
                                                out=dst[s, :, jj, g * TG:(g + 1) * TG],
                                                in_=stf_g[fslot][:]))
                                            stfbuf.use(fslot, SPa.tok(bi))
                                    else:
                                        @block.scalar
                                        def _(a):
                                            em.wait(a, "T", t_mm)
                                            stbuf.wait_reuse(a, ss)
                                            a.drain()
                                            a.activation(st_g[ss][:, 0, :], pb[:], AF.Identity,
                                                         bias=bias_cols[:, jj:jj + 1])
                                            em.inc("A", a.drain())
                                        a_hi = em.cnt["A"]

                                        @block.vector
                                        def _(v):
                                            em.wait(v, "A", a_hi)
                                            v.drain()
                                            em.inc("V", v.scalar_tensor_tensor(
                                                st_g[ss][:, 1, :], pb[:], bias_cols[:, jj:jj + 1],
                                                st_g[ss][:, 0, :], ALU.add, ALU.subtract))
                                            pbuf.use(pslot, em.tok("V"))
                                        v_lo = em.cnt["V"]

                                        sslot = st_i % 4

                                        @block.scalar
                                        def _(a):
                                            em.wait(a, "V", v_lo)
                                            bi = SPa.batch(a)
                                            SPa.inc(bi, a.dma_start(
                                                out=dst[s, jj, :, :, g * TG:(g + 1) * TG]
                                                .rearrange("l p t -> p l t"),
                                                in_=st_g[ss][:]))
                                            stbuf.use(ss, SPa.tok(bi))
                                            sqbuf.wait_reuse(a, sslot)
                                            em.inc("A", a.activation(sq_g[sslot][:], st_g[ss][:, 0, :],
                                                                     AF.Square))
                                            stbuf.use(ss, em.tok("A"))
                                        pending_norm.append((em.cnt["A"], sslot,
                                                             (s * H + jj) * NG + g, normbuf))
                                    st_i += 1
                            else:
                                for tc in range(TPG):
                                    for ot in range(WH * 128 // OT):
                                        pslot = st_i % 2
                                        pb = ps_p[pslot]

                                        @block.tensor
                                        def _(t):
                                            wait_toks(t, [tok_w, tok_a])
                                            pbuf.wait_reuse(t, pslot)
                                            first = True
                                            for c in range(NCH):
                                                ah = act_g[aslot][:, c, 0, tc * 128:(tc + 1) * 128]
                                                al = act_g[aslot][:, c, 1, tc * 128:(tc + 1) * 128]
                                                wh = wt_sb[:, c, 0, ot * OT:(ot + 1) * OT]
                                                wl = wt_sb[:, c, 1, ot * OT:(ot + 1) * OT]
                                                t.matmul(pb[:, 0:OT], ah, wh, start=first, stop=False)
                                                t.matmul(pb[:, 0:OT], al, wh, start=False, stop=False)
                                                mi = t.matmul(pb[:, 0:OT], ah, wl, start=False,
                                                              stop=(c == NCH - 1))
                                                first = False
                                            em.inc("T", mi)
                                            abuf.use(aslot, em.tok("T"))
                                            wbuf.use(wslot, em.tok("T"))
                                        t_mm = em.cnt["T"]
                                        vslot = 0
                                        ss = st_i % 4

                                        @block.vector
                                        def _(v):
                                            em.wait(v, "T", t_mm)
                                            stvbuf.wait_reuse(v, vslot)
                                            stbuf.wait_reuse(v, ss)
                                            v.drain()
                                            obase = half * WH * 128 + ot * OT
                                            v.tensor_tensor(stv_g[vslot][:, 0:OT], pb[:, 0:OT],
                                                            vbb[:, obase:obase + OT], ALU.add)
                                            v.drain()
                                            v.tensor_copy(st_g[ss][:, 0, 0:OT], stv_g[vslot][:, 0:OT])
                                            v.drain()
                                            em.inc("V", v.tensor_tensor(
                                                st_g[ss][:, 1, 0:OT], stv_g[vslot][:, 0:OT],
                                                st_g[ss][:, 0, 0:OT], ALU.subtract))
                                            pbuf.use(pslot, em.tok("V"))
                                            stvbuf.use(vslot, em.tok("V"))
                                        v_ev = em.cnt["V"]

                                        @block.gpsimd
                                        def _(gp):
                                            em.wait(gp, "V", v_ev)
                                            a_blk = (g * TG + tc * 128) // 128
                                            bi = SPg.batch(gp)
                                            for hh in range(OT // HD):
                                                h = (half * WH * 128 + ot * OT) // HD + hh
                                                SPg.inc(bi, gp.dma_start(
                                                    out=sc_V[s, h, :, a_blk, :]
                                                    .rearrange("p (l d) -> p l d", l=2),
                                                    in_=st_g[ss][:, :, hh * HD:(hh + 1) * HD]))
                                            stbuf.use(ss, SPg.tok(bi))
                                        st_i += 1
            emit_pending_norms(drainall=True)

        phase_barrier()

        with contextlib.ExitStack() as pp:
            q_sb = [sb(f"p2_q{i}", [128, 2, TG], BF16, stack=pp) for i in range(2)]
            k_sb = [sb(f"p2_k{i}", [128, 2, S], BF16, stack=pp) for i in range(2)]
            v_sb = [sb(f"p2_v{i}", [128, NKT, 2 * HD], BF16, stack=pp) for i in range(2)]
            atf = [sb(f"p2_atf{i}", [128, KG * TG], F32, stack=pp) for i in range(4)]
            atp = [sb(f"p2_atp{i}", [128, 2, KG * TG], BF16, stack=pp) for i in range(4)]
            avtf = sb("p2_avtf", [128, TG], F32, stack=pp)
            avu = [sb(f"p2_avu{i}", [128, TG], F32, stack=pp) for i in range(2)]
            avt_sb = sb("p2_avt", [128, H, 2, TG], BF16, stack=pp)
            g_sb = [sb(f"p2_g{i}", [128, TG], F32, stack=pp) for i in range(2)]
            xo_sb = sb("p2_xo", [128, TPG, D], stack=pp)
            ow_sb = [sb(f"p2_ow{i}", [128, NCH, 2, 128], BF16, stack=pp) for i in range(2)]
            p_st = [sb(f"p2_pst{i}", [128, TG], F32, stack=pp) for i in range(2)]
            rc_row = sb("p2_rc", [1, TG], F32, stack=pp)
            rc_b = sb("p2_rcb", [128, TG], F32, stack=pp)
            mask01 = sb("mask01_s", [128, MW], stack=pp)
            NSLOT = 2
            ps_sc = pp.enter_context(nc.psum_tensor("p2_sc", [128, NSLOT * KG * TG], F32))
            ps_av2 = [pp.enter_context(nc.psum_tensor(f"p2_av{i}", [128, TG], F32)) for i in range(2)]
            ps_sum2 = [pp.enter_context(nc.psum_tensor(f"p2_sum{i}", [1, TG], F32)) for i in range(2)]
            negc_bb = sb("p2_negbb", [128, NSEQ * H], stack=pp)
            kvbuf = Buf(2)
            qbuf = Buf(2)
            aubuf = Buf(2)
            rcrbuf = Buf(1)
            atfbuf = Buf(4)
            atpbuf = Buf(4)
            owbuf = Buf(2)
            pstbuf = Buf(2)
            gbuf = Buf(2)
            xgbuf = Buf(1)

            for s in range(NSEQ):
                for h in range(H):
                    idx = s * H + h

                    @block.vector
                    def _(v):
                        em.wait(v, "A")
                        v.drain()
                        base = idx * NG
                        v.tensor_reduce(spool[:, 1:2], normq[:, base:base + NG], axis=AX.X, op=ALU.max)
                        v.tensor_reduce(spool[:, 2:3], normk[:, base:base + NG], axis=AX.X, op=ALU.max)
                        v.drain()
                        em.inc("V", v.tensor_tensor(spool[:, 3:4], spool[:, 1:2],
                                                    spool[:, 2:3], ALU.mult))
                    vv = em.cnt["V"]

                    @block.scalar
                    def _(a):
                        em.wait(a, "V", vv)
                        a.drain()
                        em.inc("A", a.activation(spool[:, 4:5], spool[:, 3:4], AF.Sqrt))
                    aa = em.cnt["A"]

                    @block.vector
                    def _(v):
                        em.wait(v, "A", aa)
                        v.drain()
                        em.inc("V", v.tensor_scalar(negC[:, idx:idx + 1], spool[:, 4:5],
                                                    -cfg.SCALE, None, ALU.mult))

            @block.scalar
            def _(a):
                em.wait(a, "V")
                bi = SPa.batch(a)
                SPa.inc(bi, a.dma_start(out=sc_negc[:], in_=negC[:]))
                cell["t"] = SPa.tok(bi)
            tok_ncst = cell["t"]

            @block.sync
            def _(sync):
                wait_toks(sync, [tok_ncst])
                bi = LP.batch(sync)
                LP.inc(bi, sync.dma_start(out=negc_bb[:],
                                          in_=sc_negc.partition_broadcast(128)
                                          .rearrange("p a t -> p (a t)")))
                cell["t"] = LP.tok(bi)
            tok_ncbb = cell["t"]

            @block.sync
            def _(sync):
                bi = LP.batch(sync)
                LP.inc(bi, sync.dma_start(out=mask01[:], in_=mask_h[:]))
                cell["t"] = LP.tok(bi)
            tok_mask = cell["t"]

            @block.vector
            def _(v):
                wait_toks(v, [tok_mask])
                em.inc("V", v.drain())

            epfree = [None]
            for s in range(NSEQ):
                for m in range(NMT):
                    block.flush()
                    nkt = (m + 1) * TG // 128
                    ngrp = (nkt + KG - 1) // KG

                    @block.sync
                    def _(sync):
                        xgbuf.wait_reuse(sync, 0)
                        bi = LP.batch(sync)
                        LP.inc(bi, sync.dma_start(
                            out=xo_sb[:], in_=x_h[s, m * TG:(m + 1) * TG, :]
                            .rearrange("(a p) d -> p a d", p=128)))
                        cell["t"] = LP.tok(bi)
                    tok_xg = cell["t"]

                    def load_kv(h):
                        kvslot = h % 2

                        @block.sync
                        def _(sync):
                            kvbuf.wait_reuse(sync, kvslot)
                            qbuf.wait_reuse(sync, kvslot)
                            bi = LP.batch(sync)
                            LP.inc(bi, sync.dma_start(
                                out=q_sb[kvslot][:], in_=sc_QT[s, h, :, :, m * TG:(m + 1) * TG]
                                .rearrange("l p t -> p l t")))
                            LP.inc(bi, sync.dma_start(
                                out=k_sb[kvslot][:, :, 0:nkt * 128],
                                in_=sc_KT[s, h, :, :, 0:nkt * 128]
                                .rearrange("l p t -> p l t")))
                            LP.inc(bi, sync.dma_start(
                                out=v_sb[kvslot][:, 0:nkt, :],
                                in_=sc_V[s, h, :, 0:nkt, :]))
                            cell["t"] = LP.tok(bi)
                        return cell["t"]

                    def emit_scores(h, kg, gi, tok_kv):
                        kvslot = h % 2
                        j0 = kg * KG
                        j1 = min(nkt, j0 + KG)
                        gslot = gi % 2
                        base = gslot * KG * TG

                        @block.tensor
                        def _(t):
                            wait_toks(t, [tok_kv])
                            if gi < 2 and epfree[0] is not None:
                                wait_toks(t, [epfree[0]])
                            if exp_toks[gslot] is not None:
                                wait_toks(t, [exp_toks[gslot]])
                            if kg == 0:
                                if avfree[h % 2] is not None:
                                    wait_toks(t, [avfree[h % 2], sumfree[h % 2]])
                            for j in range(j0, j1):
                                off = base + (j - j0) * TG
                                kh = k_sb[kvslot][:, 0, j * 128:(j + 1) * 128]
                                kl = k_sb[kvslot][:, 1, j * 128:(j + 1) * 128]
                                t.matmul(ps_sc[:, off:off + TG], kh, q_sb[kvslot][:, 0, :],
                                         start=True, stop=False)
                                t.matmul(ps_sc[:, off:off + TG], kh, q_sb[kvslot][:, 1, :],
                                         start=False, stop=False)
                                em.inc("T", t.matmul(ps_sc[:, off:off + TG], kl,
                                                     q_sb[kvslot][:, 0, :],
                                                     start=False, stop=True))
                        return em.tok("T")

                    def emit_exp(h, kg, gi, t_sc):
                        j0 = kg * KG
                        j1 = min(nkt, j0 + KG)
                        w = j1 - j0
                        gslot = gi % 2
                        aslot = gi % 4
                        base = gslot * KG * TG

                        @block.scalar
                        def _(a):
                            wait_toks(a, [t_sc, tok_ncbb])
                            atfbuf.wait_reuse(a, aslot)
                            a.drain()
                            em.inc("A", a.activation(atf[aslot][:, 0:w * TG],
                                                     ps_sc[:, base:base + w * TG], AF.Exp,
                                                     scale=cfg.SCALE,
                                                     bias=negc_bb[:, s * H + h:s * H + h + 1]))
                        a_exp = em.cnt["A"]
                        exp_toks[gslot] = (em.sems["A"], a_exp)

                        masked = [j for j in range(j0, j1) if j - m * TPG >= 0]
                        g_mask = None
                        if masked:
                            @block.gpsimd
                            def _(gp):
                                em.wait(gp, "A", a_exp)
                                wait_toks(gp, [tok_mask])
                                for j in masked:
                                    jj = j - m * TPG
                                    off = (TG - 128) - 128 * jj
                                    fo = (j - j0) * TG
                                    mi = gp.tensor_tensor(atf[aslot][:, fo:fo + TG],
                                                          atf[aslot][:, fo:fo + TG],
                                                          mask01[:, off:off + TG], ALU.mult)
                                em.inc("G", mi)
                            g_mask = em.cnt["G"]

                        @block.scalar
                        def _(a):
                            if g_mask is not None:
                                em.wait(a, "G", g_mask)
                            atpbuf.wait_reuse(a, aslot)
                            a.drain()
                            em.inc("A", a.activation(atp[aslot][:, 0, 0:w * TG],
                                                     atf[aslot][:, 0:w * TG], AF.Identity))
                        a_hi = em.cnt["A"]

                        @block.vector
                        def _(v):
                            em.wait(v, "A", a_hi)
                            v.drain()
                            em.inc("V", v.tensor_tensor(atp[aslot][:, 1, 0:w * TG],
                                                        atf[aslot][:, 0:w * TG],
                                                        atp[aslot][:, 0, 0:w * TG],
                                                        ALU.subtract))
                            atfbuf.use(aslot, em.tok("V"))
                        return em.tok("V")

                    def emit_av(h, kg, gi, split_tok):
                        kvslot = h % 2
                        j0 = kg * KG
                        j1 = min(nkt, j0 + KG)
                        aslot = gi % 4
                        pav = ps_av2[h % 2]
                        psum_ = ps_sum2[h % 2]

                        @block.tensor
                        def _(t):
                            wait_toks(t, [split_tok])
                            for j in range(j0, j1):
                                fo = (j - j0) * TG
                                ah = atp[aslot][:, 0, fo:fo + TG]
                                al = atp[aslot][:, 1, fo:fo + TG]
                                vh = v_sb[kvslot][:, j, 0:HD]
                                vl = v_sb[kvslot][:, j, HD:2 * HD]
                                first = (j == 0)
                                last = (j == nkt - 1)
                                t.matmul(psum_[:], onesb[:], ah,
                                         start=first, stop=False, skip_group_check=True)
                                em.inc("T", t.matmul(psum_[:], onesb[:], al,
                                                     start=False, stop=last,
                                                     skip_group_check=True))
                                t.matmul(pav[:], vh, ah, start=first, stop=False,
                                         skip_group_check=True)
                                t.matmul(pav[:], vh, al, start=False, stop=False,
                                         skip_group_check=True)
                                em.inc("T", t.matmul(pav[:], vl, ah, start=False, stop=last,
                                                     skip_group_check=True))
                            atpbuf.use(aslot, em.tok("T"))
                            kvbuf.use(kvslot, em.tok("T"))
                            qbuf.use(kvslot, em.tok("T"))
                        return em.tok("T")

                    def head_tail(h, t_av):
                        auslot = h % 2
                        pav = ps_av2[h % 2]
                        psum_ = ps_sum2[h % 2]

                        @block.scalar
                        def _(a):
                            wait_toks(a, [t_av])
                            aubuf.wait_reuse(a, auslot)
                            a.drain()
                            em.inc("A", a.activation(avu[auslot][:], pav[:], AF.Identity))
                        avfree[h % 2] = em.tok("A")

                        @block.vector
                        def _(v):
                            wait_toks(v, [t_av])
                            rcrbuf.wait_reuse(v, 0)
                            v.drain()
                            em.inc("V", v.reciprocal(rc_row[:], psum_[:]))
                        v_rc = em.cnt["V"]
                        sumfree[h % 2] = em.tok("V")

                        @block.gpsimd
                        def _(gp):
                            em.wait(gp, "V", v_rc)
                            bi = SPg.batch(gp)
                            SPg.inc(bi, gp.dma_start(out=sc_rc[s, m, h, :].unsqueeze(0),
                                                     in_=rc_row[:]))
                            cell["t"] = SPg.tok(bi)
                        tok_rcst = cell["t"]
                        rcrbuf.use(0, tok_rcst)

                        @block.gpsimd
                        def _(gp):
                            wait_toks(gp, [tok_rcst])
                            bi = SPg.batch(gp)
                            SPg.inc(bi, gp.dma_start(
                                out=rc_b[:], in_=sc_rc[s, m, h, :]
                                .unsqueeze(0).partition_broadcast(128)
                                .rearrange("p a t -> p (a t)")))
                            cell["t"] = SPg.tok(bi)
                        tok_rcb = cell["t"]

                        @block.gpsimd
                        def _(gp):
                            wait_toks(gp, [tok_rcb, avfree[h % 2]])
                            gp.tensor_tensor(avtf[:], avu[auslot][:], rc_b[:], ALU.mult)
                            gp.drain()
                            em.inc("G", gp.tensor_copy(avt_sb[:, h, 0, :], avtf[:]))
                            aubuf.use(auslot, em.tok("G"))
                            gp.drain()
                            em.inc("G", gp.tensor_tensor(avt_sb[:, h, 1, :], avtf[:],
                                                         avt_sb[:, h, 0, :], ALU.subtract))

                    exp_toks = [None, None]
                    avfree = [None, None]
                    sumfree = [None, None]
                    pend = []
                    gi = 0

                    def pop_one():
                        hh, kk, gg_, sp = pend.pop(0)
                        t_av = emit_av(hh, kk, gg_, sp)
                        if kk == ngrp - 1:
                            head_tail(hh, t_av)

                    for h in range(H):
                        while pend and pend[0][0] <= h - 2:
                            pop_one()
                        tok_kv = load_kv(h)
                        for kg in range(ngrp):
                            t_sc = emit_scores(h, kg, gi, tok_kv)
                            if len(pend) >= 3:
                                pop_one()
                            pend.append((h, kg, gi, emit_exp(h, kg, gi, t_sc)))
                            gi += 1
                    while pend:
                        pop_one()
                    v_avt = em.cnt["V"]
                    g_avt = em.cnt["G"]
                    block.flush()

                    for j in range(NCH):
                        owslot = j % 2
                        gslot = j % 2

                        @block.sync
                        def _(sync):
                            owbuf.wait_reuse(sync, owslot)
                            gbuf.wait_reuse(sync, gslot)
                            bi = LP.batch(sync)
                            LP.inc(bi, sync.dma_start(out=ow_sb[owslot][:], in_=sc_oT[j]))
                            LP.inc(bi, sync.dma_start(
                                out=g_sb[gslot][:],
                                in_=sc_g[s, :, j, m * TG:(m + 1) * TG]))
                            cell["t"] = LP.tok(bi)
                        tok_ow = cell["t"]
                        pob = ps_sc[:, (j % 2) * TG:(j % 2 + 1) * TG]

                        @block.tensor
                        def _(t):
                            wait_toks(t, [tok_ow])
                            em.wait(t, "V", v_avt)
                            em.wait(t, "G", g_avt)
                            em.wait(t, "V")
                            first = True
                            for c in range(NCH):
                                oh = ow_sb[owslot][:, c, 0, :]
                                ol = ow_sb[owslot][:, c, 1, :]
                                ah = avt_sb[:, c, 0, :]
                                al = avt_sb[:, c, 1, :]
                                t.matmul(pob, oh, ah, start=first, stop=False)
                                t.matmul(pob, oh, al, start=False, stop=False)
                                mi = t.matmul(pob, ol, ah, start=False, stop=(c == NCH - 1))
                                first = False
                            em.inc("T", mi)
                            owbuf.use(owslot, em.tok("T"))
                        t_o = em.cnt["T"]
                        pslot = j % 2

                        @block.vector
                        def _(v):
                            em.wait(v, "T", t_o)
                            wait_toks(v, [tok_xg])
                            pstbuf.wait_reuse(v, pslot)
                            v.drain()
                            em.inc("V", v.scalar_tensor_tensor(p_st[pslot][:], pob,
                                                               ob[:, j:j + 1], g_sb[gslot][:],
                                                               ALU.add, ALU.mult))
                            xgbuf.use(0, em.tok("V"))
                            gbuf.use(gslot, em.tok("V"))
                        v_p = em.cnt["V"]

                        @block.tensor
                        def _(t):
                            em.wait(t, "V", v_p)
                            for tt in range(TPG):
                                t.transpose(ps_sc[:, 2 * TG + tt * 128:2 * TG + (tt + 1) * 128],
                                            p_st[pslot][:, tt * 128:(tt + 1) * 128], ident[:])
                            em.inc("T", t.drain())
                            pstbuf.use(pslot, em.tok("T"))
                        t_tp = em.cnt["T"]

                        @block.vector
                        def _(v):
                            em.wait(v, "T", t_tp)
                            v.drain()
                            em.inc("V", v.tensor_tensor(
                                xo_sb[:, :, j * 128:(j + 1) * 128],
                                ps_sc[:, 2 * TG:3 * TG].rearrange("p (a f) -> p a f", a=TPG),
                                xo_sb[:, :, j * 128:(j + 1) * 128],
                                ALU.add))
                            xgbuf.use(0, em.tok("V"))
                    v_out = em.cnt["V"]
                    epfree[0] = em.tok("V")

                    @block.gpsimd
                    def _(gp):
                        em.wait(gp, "V", v_out)
                        bi = OP.batch(gp)
                        OP.inc(bi, gp.dma_start(
                            out=y_h[s, m * TG:(m + 1) * TG, :]
                            .rearrange("(a p) d -> p a d", p=128),
                            in_=xo_sb[:]))
                        xgbuf.use(0, OP.tok(bi))

            @block.sync
            def _(sync):
                OP.wait_all(sync)
            block.flush()

    return nc



_CACHE = {}


def _mask01(TG):
    MW = 2 * TG - 128
    p = np.arange(128)[:, None]
    c = np.arange(MW)[None, :]
    return (c - (TG - 128) >= p).astype(np.float32)


def _chunk_cols(v, NCH):
    return np.ascontiguousarray(np.asarray(v, np.float32).reshape(NCH, 128).T)


def make_in_maps(cfg, x, q_w, q_b, k_w, k_b, v_w, v_b, o_w, o_b, W, ln_w, ln_b,
                 ncores):
    NCH = cfg.NCH
    shared = {
        "qT": np.ascontiguousarray(np.asarray(q_w, np.float32).T),
        "kT": np.ascontiguousarray(np.asarray(k_w, np.float32).T),
        "vT": np.ascontiguousarray(np.asarray(v_w, np.float32).T),
        "oT": np.ascontiguousarray(np.asarray(o_w, np.float32).T),
        "W": np.ascontiguousarray(np.asarray(W, np.float32)),
        "qb": _chunk_cols(q_b, NCH),
        "kb": _chunk_cols(k_b, NCH),
        "ob": _chunk_cols(o_b, NCH),
        "vbrow": np.ascontiguousarray(np.asarray(v_b, np.float32).reshape(1, -1)),
        "lnw": _chunk_cols(ln_w, NCH),
        "lnb": _chunk_cols(ln_b, NCH),
        "id128": np.eye(128, dtype=np.float32),
        "mask01": _mask01(cfg.TG),
    }
    x = np.ascontiguousarray(np.asarray(x, np.float32))
    per = x.shape[0] // ncores
    return [dict(shared, x=np.ascontiguousarray(x[i * per:(i + 1) * per]))
            for i in range(ncores)]


def kernel(L0_attn_x, L0_attn_q_w, L0_attn_q_b, L0_attn_k_w, L0_attn_k_b,
           L0_attn_v_w, L0_attn_v_b, L0_attn_o_w, L0_attn_o_b,
           L0_ffn_W, L0_ffn_ln_w, L0_ffn_ln_b):
    cfg = Cfg()
    if "nc" not in _CACHE:
        _CACHE["nc"] = build(cfg)
    in_maps = make_in_maps(cfg, L0_attn_x, L0_attn_q_w, L0_attn_q_b, L0_attn_k_w,
                           L0_attn_k_b, L0_attn_v_w, L0_attn_v_b, L0_attn_o_w,
                           L0_attn_o_b, L0_ffn_W, L0_ffn_ln_w, L0_ffn_ln_b, NCORES)
    res = run_bass_kernel_spmd(_CACHE["nc"], in_maps, core_ids=list(range(NCORES)))
    return np.concatenate([np.asarray(r["y"], np.float32) for r in res.results], axis=0)
